# version 06 (final)
# baseline (speedup 1.0000x reference)
import contextlib
import math
import numpy as np
import concourse.bacc as bacc
import concourse.bass as bass
import concourse.mybir as mybir
from concourse.bass_utils import run_bass_kernel_spmd

F32 = mybir.dt.float32
BF16 = mybir.dt.bfloat16
AF = mybir.ActivationFunctionType
ALU = mybir.AluOpType
AX = mybir.AxisListType

D = 4096
T = 4096
NH = 32
NKV = 8
TT = 1024
NDC = 32
SHIFT = 12.0
SCALE = 1.0 / math.sqrt(128.0)
EPS = 1e-5
QS, QE = 611, T - 1
PI = math.pi
TWO_PI = 2.0 * math.pi
NWAVE = 32
RING = 16
TWO_PI_C1 = 6.28125
TWO_PI_C2 = float(np.float32(2.0 * math.pi - 6.28125))
TWO_PI_C3 = float(2.0 * math.pi - 6.28125 - np.float32(2.0 * math.pi - 6.28125))


_DEBUG = {"dump": False}
_ATTN_GROUPS = [8]

ENGS = ("tensor", "scalar", "vector", "sync", "gpsimd")


_OPN = [0]


class Op:
    __slots__ = ("eng", "fn", "deps", "dma", "sem", "order")

    def __init__(self, eng, fn, deps, dma):
        self.eng = eng
        self.fn = fn
        assert all(d is not None for d in deps), "None dep (forward reference?)"
        self.deps = list(deps)
        self.dma = dma
        self.sem = None
        self.order = _OPN[0]
        _OPN[0] += 1


class Prog:
    def __init__(self):
        self.segments = []
        self.forced = set()
        self._cur = None
        self._open_plain()

    def _open_plain(self):
        self._cur = {e: [] for e in ENGS}
        self.segments.append({"kind": "plain", "ops": self._cur})

    def add(self, eng, fn, deps=(), dma=False):
        op = Op(eng, fn, deps, dma)
        self._cur[eng].append(op)
        return op

    def observe(self, op):
        self.forced.add(id(op))

    @contextlib.contextmanager
    def switch(self, n, engines, mod):
        seg = {"kind": "switch", "n": n, "engines": list(engines), "mod": mod,
               "bodies": [{e: [] for e in ENGS} for _ in range(n)]}
        self.segments.append(seg)
        saved = self._cur

        def body(k):
            self._cur = seg["bodies"][k]
        try:
            yield body
        finally:
            self._cur = saved
            self._open_plain()

    DMA_POOL = {"tensor": 2, "scalar": 6, "vector": 4, "sync": 8, "gpsimd": 8}

    def schedule(self):
        observed = set(self.forced)

        def mark(ops_dict):
            for e in ENGS:
                for op in ops_dict[e]:
                    if op.dma:
                        observed.add(id(op))
                    for d in op.deps:
                        if d.eng != op.eng or d.dma:
                            observed.add(id(d))
        for seg in self.segments:
            if seg["kind"] == "plain":
                mark(seg["ops"])
            else:
                for b in seg["bodies"]:
                    mark(b)

        cnt = {}
        for e in ENGS:
            cnt[e + "_c"] = 0
            for i in range(self.DMA_POOL[e]):
                cnt[f"{e}_d{i}"] = 0
            cnt[e + "_dn"] = 0

        def walk(ops_dict, counters):
            for e in ENGS:
                for op in ops_dict[e]:
                    if id(op) in observed:
                        if op.dma:
                            i = counters[e + "_dn"] % self.DMA_POOL[e]
                            counters[e + "_dn"] += 1
                            nm = f"{e}_d{i}"
                            counters[nm] += 16
                            op.sem = (nm, counters[nm])
                        else:
                            nm = e + "_c"
                            counters[nm] += 1
                            op.sem = (nm, counters[nm])

        for seg in self.segments:
            if seg["kind"] == "plain":
                walk(seg["ops"], cnt)
            else:
                finals = []
                for b in seg["bodies"]:
                    c2 = dict(cnt)
                    walk(b, c2)
                    finals.append(c2)
                for c2 in finals[1:]:
                    assert c2 == finals[0], (
                        f"non-uniform switch tick counts: {finals[0]} vs {c2}")
                cnt = finals[0]
        self.final = dict(cnt)

    def emit(self, nc, blk, sems):
        ENGOBJ = {"tensor": nc.tensor, "scalar": nc.scalar, "vector": nc.vector,
                  "sync": nc.sync, "gpsimd": nc.gpsimd}

        def emit_op(op):
            E = ENGOBJ[op.eng]
            waits = {}
            for d in op.deps:
                if d.eng == op.eng and not d.dma:
                    continue
                assert d.sem is not None, "dep never scheduled"
                nm, v = d.sem
                waits[nm] = max(waits.get(nm, 0), v)
            if op.sem is not None and op.dma:
                nm, v = op.sem
                if v > 16:
                    waits[nm] = max(waits.get(nm, 0), v - 16)
            for nm, v in waits.items():
                E.wait_ge(sems[nm], v)
            r = op.fn(E)
            if r is None:
                instrs = []
            elif isinstance(r, (list, tuple)):
                instrs = [i for i in r if i is not None]
            else:
                instrs = [r]
            if instrs:
                if op.sem is not None:
                    nm, _ = op.sem
                    instrs[-1].then_inc(sems[nm], 16 if op.dma else 1)
            else:
                assert op.sem is None

        def em(_):
            for seg in self.segments:
                if seg["kind"] == "plain":
                    for e in ENGS:
                        for op in seg["ops"][e]:
                            emit_op(op)
                else:
                    engs = [e for e in ENGS
                            if any(seg["bodies"][k][e] for k in range(seg["n"]))]
                    if not engs:
                        continue
                    eobjs = [ENGOBJ[e] for e in engs]
                    idx = [E.partition_id() % seg["mod"] for E in eobjs]
                    for k in nc.Switch(engines=eobjs, index=idx, n=seg["n"]):
                        for e in engs:
                            for op in seg["bodies"][k][e]:
                                emit_op(op)
        blk.gpsimd(em)


def build_program(trunc=None):
    nc = bacc.Bacc("TRN2", target_bir_lowering=False, num_devices=8,
                   detect_race_conditions=False)
    st = contextlib.ExitStack()

    x_tok = nc.dram_tensor("x_tok", [TT, D], F32, kind="ExternalInput").ap()
    x_fmT = nc.dram_tensor("x_fmT", [D, TT], F32, kind="ExternalInput").ap()
    wq_l = nc.dram_tensor("wq_l", [NH, 128, D], F32, kind="ExternalInput").ap()
    wk_l = nc.dram_tensor("wk_l", [NKV, 128, D], F32, kind="ExternalInput").ap()
    wv_l = nc.dram_tensor("wv_l", [NKV, 128, D], F32, kind="ExternalInput").ap()
    wo_n = nc.dram_tensor("wo_n", [NDC, 128, D], F32, kind="ExternalInput").ap()
    wt_l = nc.dram_tensor("wt_l", [8, 128, NDC * 512], F32, kind="ExternalInput").ap()
    a_t2 = nc.dram_tensor("a_t2", [128, NDC * 128], F32, kind="ExternalInput").ap()
    ball_in = nc.dram_tensor("ball_in", [128, D], F32, kind="ExternalInput").ap()
    bias_in = nc.dram_tensor("bias_in", [1, D], F32, kind="ExternalInput").ap()
    rwt2 = nc.dram_tensor("rwt2", [128, NDC * 8], F32, kind="ExternalInput").ap()
    rb_in = nc.dram_tensor("rb_in", [1, 8], F32, kind="ExternalInput").ap()
    pos_in = nc.dram_tensor("pos_in", [128, 8], F32, kind="ExternalInput").ap()
    posb_in = nc.dram_tensor("posb_in", [128, TT], F32, kind="ExternalInput").ap()
    out_h = nc.dram_tensor("out_h", [TT, D], F32, kind="ExternalOutput").ap()

    k_all = nc.dram_tensor("k_all", [NKV, 128, T], BF16, kind="Internal").ap()
    v_all = nc.dram_tensor("v_all", [T, 1024], BF16, kind="Internal").ap()
    x1_dram = nc.dram_tensor("x1_dram", [TT, D], F32, kind="Internal").ap()
    o2_dram = nc.dram_tensor("o2_dram", [TT, D], F32, kind="Internal").ap()

    dbg = {}
    if _DEBUG["dump"]:
        dbg["q"] = nc.dram_tensor("dbg_q", [128, NH * TT], F32, kind="ExternalOutput").ap()
        dbg["kall"] = nc.dram_tensor("dbg_kall", [NKV, 128, T], F32, kind="ExternalOutput").ap()
        dbg["vall"] = nc.dram_tensor("dbg_vall", [T, 1024], F32, kind="ExternalOutput").ap()
        dbg["attn"] = nc.dram_tensor("dbg_attn", [128, NH * TT], F32, kind="ExternalOutput").ap()
        dbg["x1"] = nc.dram_tensor("dbg_x1", [TT, D], F32, kind="ExternalOutput").ap()
        dbg["rout"] = nc.dram_tensor("dbg_rout", [1, 8], F32, kind="ExternalOutput").ap()
        dbg["xfm"] = nc.dram_tensor("dbg_xfm", [128, NDC * TT], F32, kind="ExternalOutput").ap()
        dbg["trig"] = nc.dram_tensor("dbg_trig", [128, 2 * TT], F32, kind="ExternalOutput").ap()
        dbg["ring"] = nc.dram_tensor("dbg_ring", [128, RING * 512], F32, kind="ExternalOutput").ap()
        dbg["scr"] = nc.dram_tensor("dbg_scr", [128, 8], F32, kind="ExternalOutput").ap()
        dbg["tmpA"] = nc.dram_tensor("dbg_tmpA", [128, TT], F32, kind="ExternalOutput").ap()

    sems = {}
    for e in ENGS:
        sems[e + "_c"] = st.enter_context(nc.semaphore(e + "_c"))
        for i in range(Prog.DMA_POOL[e]):
            sems[f"{e}_d{i}"] = st.enter_context(nc.semaphore(f"{e}_d{i}"))
    for n in ("prep", "dsem", "asem", "lsem", "rsem", "outd"):
        sems[n] = st.enter_context(nc.semaphore(n))

    blk = st.enter_context(nc.Block())
    P = Prog()

    ident = st.enter_context(nc.sbuf_tensor("ident", [128, 128], BF16))
    identf = st.enter_context(nc.sbuf_tensor("identf", [128, 128], F32))
    rperm = st.enter_context(nc.sbuf_tensor("rperm", [128, 128], BF16))
    selm = st.enter_context(nc.sbuf_tensor("selm", [8, 128], BF16))
    onescol = st.enter_context(nc.sbuf_tensor("onescol", [128, 1], BF16))
    onesrow = st.enter_context(nc.sbuf_tensor("onesrow", [1, 128], BF16))
    b12 = st.enter_context(nc.sbuf_tensor("b12", [128, 1], F32))
    beps = st.enter_context(nc.sbuf_tensor("beps", [128, 1], F32))
    pos8 = st.enter_context(nc.sbuf_tensor("pos8", [128, 8], F32))
    qm8 = st.enter_context(nc.sbuf_tensor("qm8", [128, 8], BF16))
    scr = st.enter_context(nc.sbuf_tensor("scr", [128, 8], F32))
    scr2 = st.enter_context(nc.sbuf_tensor("scr2", [128, 8], F32))
    rtg2 = st.enter_context(nc.sbuf_tensor("rtg2", [1, 8], F32))
    recv = st.enter_context(nc.sbuf_tensor("recv", [128, 8 * 512], BF16))
    ackt = st.enter_context(nc.sbuf_tensor("ackt", [128, 8], F32))
    ackr = st.enter_context(nc.sbuf_tensor("ackr", [128, 8], F32))
    rsl = st.enter_context(nc.sbuf_tensor("rsl", [128, 64], F32))

    ph12 = contextlib.ExitStack()
    mask = ph12.enter_context(nc.sbuf_tensor("mask", [128, 1024], F32))
    q_attn = ph12.enter_context(nc.sbuf_tensor("q_attn", [128, NH * TT], BF16))

    ph1 = contextlib.ExitStack()
    trig1 = ph1.enter_context(nc.sbuf_tensor("trig1", [128, TT], F32))
    trig2 = ph1.enter_context(nc.sbuf_tensor("trig2", [128, TT], F32))
    tmpA = ph1.enter_context(nc.sbuf_tensor("tmpA", [128, TT], F32))
    tmpB = ph1.enter_context(nc.sbuf_tensor("tmpB", [128, 512], F32))
    ph0 = contextlib.ExitStack()
    kA = ph0.enter_context(nc.sbuf_tensor("kA", [128, TT], F32))
    kB = ph0.enter_context(nc.sbuf_tensor("kB", [128, TT], F32))
    ph0.close()
    x_fm = ph1.enter_context(nc.sbuf_tensor("x_fm", [128, NDC * TT], BF16))
    w_sb = [ph1.enter_context(nc.sbuf_tensor(f"w_sb{i}", [128, D], BF16)) for i in range(2)]
    ring = ph1.enter_context(nc.sbuf_tensor("ring", [128, RING * 512], BF16))
    praw = [ph1.enter_context(nc.sbuf_tensor(f"praw{i}", [128, 512], BF16)) for i in range(2)]
    vfm_all = ph1.enter_context(nc.sbuf_tensor("vfm_all", [128, 8 * 1024], BF16))
    ps1 = [ph1.enter_context(nc.psum_tensor(f"ps1_{i}", [128, 512], F32)) for i in range(2)]
    psr = [ph1.enter_context(nc.psum_tensor(f"psr_{i}", [128, 512], F32)) for i in range(2)]
    pst1 = [ph1.enter_context(nc.psum_tensor(f"pst1_{i}", [128, 128], BF16)) for i in range(2)]

    c_pos8 = P.add("gpsimd", lambda g: g.dma_start(out=pos8[:], in_=pos_in), dma=True)
    c_posb = P.add("gpsimd", lambda g: g.dma_start(out=tmpA[:], in_=posb_in), dma=True)
    c_iota = P.add("gpsimd", lambda g: [
        g.iota(ident[:], pattern=[[1, 128]], base=0, channel_multiplier=-1,
               allow_small_or_imprecise_dtypes=True),
        g.iota(identf[:], pattern=[[1, 128]], base=0, channel_multiplier=-1,
               allow_small_or_imprecise_dtypes=True),
        g.iota(rperm[:], pattern=[[-1, 128]], base=-64, channel_multiplier=1,
               allow_small_or_imprecise_dtypes=True),
        g.iota(selm[:], pattern=[[1, 128]], base=0, channel_multiplier=-16,
               allow_small_or_imprecise_dtypes=True),
        g.iota(mask[:], pattern=[[1, 1024]], base=-512, channel_multiplier=-1,
               allow_small_or_imprecise_dtypes=True),
        g.iota(trig1[:, 0:1], pattern=[[0, 1]], base=0, channel_multiplier=1,
               allow_small_or_imprecise_dtypes=True),
        g.iota(trig1[:, 1:2], pattern=[[0, 1]], base=0, channel_multiplier=1,
               allow_small_or_imprecise_dtypes=True),
    ])

    def f_consts(v):
        return [
            v.tensor_single_scalar(ident[:], ident[:], 0.0, ALU.is_equal),
            v.tensor_single_scalar(identf[:], identf[:], 0.0, ALU.is_equal),
            v.tensor_single_scalar(tmpB[:, 0:128], rperm[:], -128.0, ALU.is_equal),
            v.tensor_single_scalar(tmpB[:, 128:256], rperm[:], 0.0, ALU.is_equal),
            v.tensor_tensor(rperm[:], tmpB[:, 0:128], tmpB[:, 128:256], ALU.add),
            v.tensor_single_scalar(tmpB[0:8, 0:128], selm[:], 0.0, ALU.is_ge),
            v.tensor_single_scalar(tmpB[0:8, 128:256], selm[:], 15.0, ALU.is_le),
            v.tensor_tensor(selm[:], tmpB[0:8, 0:128], tmpB[0:8, 128:256], ALU.mult),
            v.tensor_single_scalar(trig2[:], mask[:], 0.0, ALU.is_ge),
            v.tensor_scalar(mask[:], trig2[:, 0:1024], 1.0, 1.0e30, ALU.subtract, ALU.mult),
            v.memset(onescol[:], 1.0),
            v.memset(ackt[:], 0.0),
            v.memset(rsl[:], 0.0),
            v.memset(onesrow[:], 1.0),
            v.memset(b12[:], -SHIFT),
            v.memset(beps[:], EPS),
            v.tensor_single_scalar(tmpB[:, 0:8], pos8[:], float(QS), ALU.is_ge),
            v.tensor_single_scalar(scr2[:], pos8[:], float(QE - 1), ALU.is_le),
            v.tensor_scalar(scr[:, 4:5], trig1[:, 0:1], 63.5,
                            math.log(10000.0), ALU.is_ge, ALU.mult),
            v.tensor_scalar(scr[:, 6:7], trig1[:, 1:2], 63.5, 2.0,
                            ALU.is_ge, ALU.mult),
        ]
    c_vec = P.add("vector", f_consts, deps=[c_pos8, c_iota])
    c_if = P.add("scalar", lambda s: s.activation(
        scr[:, 3:4], trig1[:, 0:1], AF.Exp, bias=scr[:, 4:5],
        scale=-math.log(10000.0) / 64.0),
        deps=[c_vec])
    c_qm8 = P.add("vector", lambda v: v.tensor_tensor(
        qm8[:], tmpB[:, 0:8], scr2[:], ALU.mult), deps=[c_if])

    RED_BITS = [512, 256, 128, 64, 32, 16, 8, 4, 2, 1]

    def red_steps(v, cur, oth, w):
        r = []
        for i, b in enumerate(RED_BITS):
            thr = float(np.float32(b * TWO_PI))
            a, o = (cur, oth) if i % 2 == 0 else (oth, cur)
            r.append(v.tensor_scalar(w[:], a[:], thr, thr, ALU.is_ge, ALU.mult))
            r.append(v.tensor_tensor(o[:], a[:], w[:], ALU.subtract))
        r.append(v.tensor_scalar(w[:], cur[:], float(PI),
                                 float(np.float32(TWO_PI)), ALU.is_ge, ALU.mult))
        r.append(v.tensor_tensor(oth[:], cur[:], w[:], ALU.subtract))
        return r

    def f_ang(v):
        return [
            v.tensor_scalar(kB[:], tmpA[:], scr[:, 3:4], None, ALU.mult),
            v.tensor_single_scalar(trig1[:], kB[:], PI / 2.0, ALU.add),
        ] + red_steps(v, trig1, tmpA, kA)
    c_ang = P.add("vector", f_ang, deps=[c_if, c_posb])
    c_sin1 = P.add("scalar", lambda s: s.activation(trig1[:], tmpA[:], AF.Sin),
                   deps=[c_ang])
    c_ang2 = P.add("vector", lambda v: [v.tensor_copy(trig2[:], kB[:])]
                   + red_steps(v, trig2, tmpA, kA), deps=[c_sin1])
    c_sin2 = P.add("scalar", lambda s: s.activation(kA[:], tmpA[:], AF.Sin),
                   deps=[c_ang2])
    c_trig = P.add("vector", lambda v: [
        v.tensor_scalar(tmpA[:], kA[:], scr[:, 6:7], None, ALU.mult),
        v.tensor_tensor(trig2[:], tmpA[:], kA[:], ALU.subtract)], deps=[c_sin2])

    plan = ([("wk", kv, th) for kv in range(NKV) for th in range(2)] +
            [("wv", c8, th) for c8 in range(NKV) for th in range(2)] +
            [("wq", h, th) for h in range(NH) for th in range(2)])
    NG = len(plan)
    rope_gis = [gi for gi, pl in enumerate(plan) if pl[0] in ("wk", "wq")]

    ld_xfm = []
    for dc in range(NDC):
        ld_xfm.append(P.add(
            "gpsimd",
            (lambda dc: lambda g: g.dma_start(
                out=x_fm[:, dc * TT:(dc + 1) * TT],
                in_=x_fmT[dc * 128:(dc + 1) * 128, :]))(dc),
            deps=[c_trig], dma=True))

    slabs = ([("wk", wk_l, kv) for kv in range(NKV)] +
             [("wv", wv_l, c8) for c8 in range(NKV)] +
             [("wq", wq_l, h) for h in range(NH)])
    mm_ops = [None] * NG
    ev_ops = [None] * NG
    rope_ops = {}
    piece_ops = [None] * NWAVE
    q_ready = {}
    ld_w = []
    for i, (kind, src, idx) in enumerate(slabs):
        deps = [c_trig]
        ld_w.append(P.add(
            "gpsimd",
            (lambda src, idx, bI: lambda g: g.dma_start(out=w_sb[bI][:], in_=src[idx]))(
                src, idx, i % 2),
            deps=deps, dma=True))

    for gi, (kind, idx, th) in enumerate(plan):
        slab = gi // 2
        deps = [ld_w[slab]] + ld_xfm + ([ev_ops[gi - 2]] if gi >= 2 else [])

        def f_mm(gi=gi, slab=slab, th=th):
            def fn(te):
                ps = ps1[gi % 2]
                return [te.matmul(ps[:], w_sb[slab % 2][:, dc * 128:(dc + 1) * 128],
                                  x_fm[:, dc * TT + th * 512: dc * TT + th * 512 + 512],
                                  start=(dc == 0), stop=(dc == NDC - 1))
                        for dc in range(NDC)]
            return fn
        mm_ops[gi] = P.add("tensor", f_mm(), deps=deps)

        if kind == "wv":
            ev_ops[gi] = P.add(
                "scalar",
                (lambda gi, idx, th: lambda s: s.copy(
                    vfm_all[:, idx * 1024 + th * 512: idx * 1024 + th * 512 + 512],
                    ps1[gi % 2][:]))(gi, idx, th),
                deps=[mm_ops[gi]])
        else:
            r = rope_gis.index(gi)
            deps_e = [mm_ops[gi]] + ([rope_ops[r - 2]] if r >= 2 else [])
            ev_ops[gi] = P.add(
                "scalar",
                (lambda gi, r: lambda s: s.copy(praw[r % 2][:], ps1[gi % 2][:]))(gi, r),
                deps=deps_e)
            deps_r = [ev_ops[gi]] + ([rope_ops[r - 2]] if r >= 2 else [])
            rot = P.add(
                "tensor",
                (lambda r: lambda te: te.matmul(psr[r % 2][:], rperm[:], praw[r % 2][:],
                                                start=True, stop=True))(r),
                deps=deps_r)
            if kind == "wk":
                dst = ring[:, (2 * idx + th) % RING * 512:
                           ((2 * idx + th) % RING) * 512 + 512]
            else:
                dst = q_attn[:, idx * TT + th * 512: idx * TT + th * 512 + 512]

            def f_rope(r=r, th=th, dst=dst):
                def fn(v):
                    tsl = slice(th * 512, th * 512 + 512)
                    return [
                        v.tensor_tensor(tmpA[:, 0:512], praw[r % 2][:], trig1[:, tsl], ALU.mult),
                        v.tensor_tensor(tmpB[:, 0:512], psr[r % 2][:], trig2[:, tsl], ALU.mult),
                        v.tensor_tensor(dst, tmpA[:, 0:512], tmpB[:, 0:512], ALU.add),
                    ]
                return fn
            rope_ops[r] = P.add("vector", f_rope(), deps=[rot])
            if kind == "wk":
                piece_ops[2 * idx + th] = rope_ops[r]
            elif th == 1:
                q_ready[idx] = rope_ops[r]

    for i in range(2, len(slabs)):
        ld_w[i].deps.append(mm_ops[2 * (i - 2) + 1])

    vt_list = []
    nvt = 0
    for w in range(16, 32):
        tt_i = (w - 16) // 2
        s_i = (w - 16) % 2
        first_evc = None
        for jj in range(4):
            c8 = 4 * s_i + jj
            deps = [ev_ops[16 + 2 * c8], ev_ops[16 + 2 * c8 + 1]]
            if nvt >= 2:
                deps.append(vt_list[nvt - 2][1])
            tsp = P.add(
                "tensor",
                (lambda c8, tt_i, bI: lambda te: te.transpose(
                    pst1[bI][:],
                    vfm_all[:, c8 * 1024 + tt_i * 128: c8 * 1024 + tt_i * 128 + 128],
                    ident[:]))(c8, tt_i, nvt % 2),
                deps=deps)

            if jj == 0:
                P.add("vector",
                      (lambda w=w: lambda v: v.wait_ge(
                          sems["lsem"], 16 * (2 * (w - RING) + 1)))(),
                      deps=[])
            evc = P.add(
                "vector",
                (lambda w=w, jj=jj, bI=nvt % 2: lambda v: v.tensor_copy(
                    ring[:, (w % RING) * 512 + jj * 128:(w % RING) * 512 + jj * 128 + 128],
                    pst1[bI][:]))(),
                deps=[tsp])
            vt_list.append((tsp, evc))
            if first_evc is None:
                first_evc = evc
            nvt += 1
        piece_ops[w] = evc

    if _DEBUG["dump"]:
        d_xfm = P.add("gpsimd", lambda g: g.dma_start(out=dbg["xfm"], in_=x_fm[:]),
                      deps=[ld_xfm[-1]], dma=True)
        d_scr = P.add("gpsimd", lambda g: g.dma_start(out=dbg["scr"], in_=scr[:]),
                      deps=[c_trig], dma=True)
        d_tmpA = P.add("gpsimd", lambda g: g.dma_start(out=dbg["tmpA"], in_=tmpA[:]),
                       deps=[c_trig], dma=True)
        d_trig1 = P.add("gpsimd", lambda g: g.dma_start(out=dbg["trig"][:, 0:TT], in_=trig1[:]),
                        deps=[c_trig], dma=True)
        d_trig = P.add("gpsimd", lambda g: g.dma_start(out=dbg["trig"][:, TT:2 * TT], in_=trig2[:]),
                       deps=[c_trig], dma=True)
        d_ring = P.add("gpsimd", lambda g: g.dma_start(out=dbg["ring"], in_=ring[:]),
                       deps=[piece_ops[15]] + [rope_ops[r] for r in range(16)], dma=True)
        vt_list[0][1].deps.append(d_ring)
    for w in range(NWAVE):
        P.observe(piece_ops[w])

    def slot_chunks(r):
        return r % 4, 7 - (r % 4)

    last_exch = None
    for w in range(NWAVE):
        def f_wave(w=w, piece=piece_ops[w]):
            def fn(g):
                nm, val = piece.sem
                g.wait_ge(sems[nm], val)
                if w >= 1:
                    g.wait_ge(sems["asem"], 16 * w)
                src = ring[:, (w % RING) * 512:(w % RING) * 512 + 512]
                r = g.partition_id()
                for k in g.Switch(r, 8):
                    g.remote_dma_broadcast(
                        recv[:, k * 512:(k + 1) * 512], src, sems["dsem"], sems["lsem"],
                        rdests=[(0, d) for d in range(8)]).then_inc(sems["prep"], 1)
                g.wait_ge(sems["prep"], 2 * w + 1)
                g.trigger_dma(count=1)
                g.wait_ge(sems["dsem"], 16 * (w + 1))
                b_reg = r // 4
                for bb in g.Switch(b_reg, 2):
                    for rr in range(4 * bb, 4 * bb + 4):
                        cA, cB = slot_chunks(rr)
                        sl = recv[:, rr * 512:(rr + 1) * 512]
                        if w < 16:
                            kv, th = w // 2, w % 2
                            c = cA if th == 0 else cB
                            g.dma_start(out=k_all[kv, :, 512 * c:512 * c + 512],
                                        in_=sl).then_inc(sems["outd"], 16)
                        else:
                            tt_i, s_i = (w - 16) // 2, (w - 16) % 2
                            c = cA if tt_i < 4 else cB
                            row0 = 512 * c + 128 * (tt_i % 4)
                            g.dma_start(out=v_all[row0:row0 + 128, 512 * s_i:512 * s_i + 512],
                                        in_=sl).then_inc(sems["outd"], 16)
                g.wait_ge(sems["outd"], 16 * 4 * (w + 1))
                ack = g.remote_dma_broadcast(ackr[:], ackt[:], sems["asem"], sems["lsem"],
                                             rdests=[(0, d) for d in range(8)])
                ack.then_inc(sems["prep"], 1)
                g.wait_ge(sems["prep"], 2 * w + 2)
                g.trigger_dma(count=1)
                return None
            return fn
        last_exch = P.add("gpsimd", f_wave(), deps=[])

    exch_done = P.add("gpsimd",
                      lambda g: [g.wait_ge(sems["asem"], 16 * NWAVE),
                                 g.wait_ge(sems["lsem"], 16 * 2 * NWAVE),
                                 g.engine_nop()][-1],
                      deps=[])

    ph1.close()
    ph2 = contextlib.ExitStack()
    kg = [ph2.enter_context(nc.sbuf_tensor(f"kg{i}", [128, 8 * 512], BF16)) for i in range(2)]
    vg = [ph2.enter_context(nc.sbuf_tensor(f"vg{i}", [128, 8 * 512], BF16)) for i in range(2)]
    pt = [ph2.enter_context(nc.sbuf_tensor(f"pt{i}", [128, 512], BF16)) for i in range(3)]
    a_un = ph2.enter_context(nc.sbuf_tensor("a_un", [128, 512], BF16))
    a_tk = ph2.enter_context(nc.sbuf_tensor("a_tk", [128, 128], BF16))
    invd = ph2.enter_context(nc.sbuf_tensor("invd", [1, 1024], F32))
    invq4 = [ph2.enter_context(nc.sbuf_tensor(f"invq4_{i}", [128, 4], F32)) for i in range(2)]
    psS = [ph2.enter_context(nc.psum_tensor(f"psS{i}", [128, 512], F32)) for i in range(2)]
    psA = [ph2.enter_context(nc.psum_tensor(f"psA{i}", [128, 512], F32)) for i in range(2)]
    psDn = ph2.enter_context(nc.psum_tensor("psDn", [1, 512], F32))
    psT = ph2.enter_context(nc.psum_tensor("psT", [128, 128], BF16))
    psF = ph2.enter_context(nc.psum_tensor("psF", [128, 128], BF16))
    psDq = ph2.enter_context(nc.psum_tensor("psDq", [128, 4], F32))


    MAXC = [8, 7, 6, 5]
    attn_last = None
    ld_grp = []
    first_ld = {}
    for gI in range(_ATTN_GROUPS[0]):
        deps0 = [exch_done, rope_ops[79]] if gI < 2 else []
        grp_ops = []
        for c in range(8):
            op_ = P.add(
                "sync",
                (lambda gI, c: lambda sy: sy.dma_start(
                    out=kg[gI % 2][:, c * 512:(c + 1) * 512],
                    in_=k_all[gI, :, 512 * c:512 * c + 512]))(gI, c),
                deps=(deps0 if c == 0 else []), dma=True)
            grp_ops.append(op_)
            if c == 0:
                first_ld[gI] = op_
        for c in range(8):
            for j in range(4):
                grp_ops.append(P.add(
                    "sync",
                    (lambda gI, c, j: lambda sy: sy.dma_start(
                        out=vg[gI % 2][:, c * 512 + j * 128: c * 512 + j * 128 + 128],
                        in_=v_all[512 * c + 128 * j: 512 * c + 128 * j + 128,
                                  128 * gI:128 * gI + 128]))(gI, c, j),
                    deps=[], dma=True))
        ld_grp.append(grp_ops)
    with P.switch(4, ["tensor", "scalar", "vector"], 4) as body:
        for p in range(4):
            body(p)
            nch = 8
            att_ops = {}
            aw_ops = {}
            nrm_ops = {}
            dq_ops = {}
            uo_ops = {}
            rt_ops = {}
            fw_ops = {}
            exp_all = {}
            nS = 0
            natt = 0
            for gI in range(_ATTN_GROUPS[0]):
                for hh in range(4):
                    h = 4 * gI + hh
                    for (qsel, cq) in ((0, p), (1, 7 - p)):
                        qoff = h * TT + qsel * 512
                        natt += 1
                        nsub = 4 * (cq + 1)
                        si = 0
                        av_first = list(ld_grp[gI])
                        if natt >= 3:
                            av_first.append(aw_ops[natt - 2])
                        if natt >= 2:
                            av_first.append(nrm_ops[natt - 1])
                        av_last = None
                        for c in range(cq + 1):
                            for j in range(4):
                                nS += 1
                                diag = (c == cq)
                                deps_s = []
                                if si == 0:
                                    deps_s = ld_grp[gI] + [q_ready[h]]
                                if nS >= 3:
                                    deps_s.append(exp_all[nS - 2])
                                s_op = P.add(
                                    "tensor",
                                    (lambda gI, c, j, qoff, bI: lambda te: te.matmul(
                                        psS[bI][:],
                                        kg[gI % 2][:, c * 512 + j * 128: c * 512 + j * 128 + 128],
                                        q_attn[:, qoff:qoff + 512],
                                        start=True, stop=True))(gI, c, j, qoff, (nS - 1) % 2),
                                    deps=deps_s)
                                pre = s_op
                                if diag:
                                    pre = P.add(
                                        "vector",
                                        (lambda j, bI: lambda v: v.tensor_tensor(
                                            psS[bI][:], psS[bI][:],
                                            mask[:, 512 - 128 * j: 1024 - 128 * j],
                                            ALU.add))(j, (nS - 1) % 2),
                                        deps=[s_op])
                                e_op = P.add(
                                    "scalar",
                                    (lambda bI, bp: lambda s: s.activation(
                                        pt[bp][:], psS[bI][:], AF.Exp, bias=b12[:],
                                        scale=SCALE))((nS - 1) % 2, (nS - 1) % 3),
                                    deps=[pre])
                                exp_all[nS] = e_op
                                av = P.add(
                                    "tensor",
                                    (lambda gI, c, j, ba, bp, st_, sp_: lambda te: [
                                        te.matmul(psA[ba][:],
                                                  vg[gI % 2][:, c * 512 + j * 128:
                                                             c * 512 + j * 128 + 128],
                                                  pt[bp][:], start=st_, stop=sp_),
                                        te.matmul(psDn[:], onescol[:], pt[bp][:],
                                                  start=st_, stop=sp_)])(
                                        gI, c, j, (natt - 1) % 2, (nS - 1) % 3,
                                        si == 0, si == nsub - 1),
                                    deps=[e_op] + (av_first if si == 0 else []))
                                av_last = av
                                si += 1
                        att_ops[natt] = av_last
                        nrm_deps = [av_last]
                        if natt >= 3:
                            nrm_deps.append(dq_ops[natt - 2])
                        nrm = P.add(
                            "vector",
                            (lambda qsel: lambda v: v.reciprocal(
                                invd[0:1, qsel * 512:qsel * 512 + 512], psDn[:]))(qsel),
                            deps=nrm_deps)
                        nrm_ops[natt] = nrm
                        dq_deps = [nrm]
                        if natt >= 2:
                            dq_deps.append(uo_ops[natt - 1])
                        dq = P.add(
                            "tensor",
                            (lambda qsel: lambda te: [te.transpose(
                                psDq[:, jq:jq + 1],
                                invd[0:1, qsel * 512 + jq * 128: qsel * 512 + jq * 128 + 128],
                                identf[0:1, 0:1]) for jq in range(4)])(qsel),
                            deps=dq_deps)
                        dq_ops[natt] = dq
                        uo_deps = [dq]
                        if natt >= 3:
                            uo_deps.append(rt_ops[(natt - 2, 3)])
                        uo = P.add(
                            "scalar",
                            (lambda bI: lambda s: s.copy(invq4[bI][:], psDq[:]))((natt - 1) % 2),
                            deps=uo_deps)
                        uo_ops[natt] = uo
                        aw_deps = [av_last]
                        if natt >= 2:
                            aw_deps.append(rt_ops[(natt - 1, 3)])
                        aw = P.add(
                            "scalar",
                            (lambda ba: lambda s: s.copy(a_un[:], psA[ba][:]))((natt - 1) % 2),
                            deps=aw_deps)
                        aw_ops[natt] = aw
                        for jq in range(4):
                            tdeps = [aw]
                            if natt > 1 or jq > 0:
                                tdeps.append(rt_ops[(natt, jq - 1)] if jq >= 1
                                             else rt_ops[(natt - 1, 3)])
                            t1 = P.add(
                                "tensor",
                                (lambda jq: lambda te: te.transpose(
                                    psT[:], a_un[:, jq * 128:jq * 128 + 128], ident[:]))(jq),
                                deps=tdeps)
                            rdeps = [t1, uo]
                            if natt > 1 or jq > 0:
                                rdeps.append(fw_ops[(natt, jq - 1)] if jq >= 1
                                             else fw_ops[(natt - 1, 3)])
                            rt = P.add(
                                "vector",
                                (lambda bI, jq: lambda v: v.tensor_scalar(
                                    a_tk[:], psT[:], invq4[bI][:, jq:jq + 1], None,
                                    ALU.mult))((natt - 1) % 2, jq),
                                deps=rdeps)
                            rt_ops[(natt, jq)] = rt
                            t2 = P.add("tensor",
                                       lambda te: te.transpose(psF[:], a_tk[:], ident[:]),
                                       deps=[rt])
                            fw = P.add(
                                "scalar",
                                (lambda qoff, jq: lambda s: s.copy(
                                    q_attn[:, qoff + jq * 128: qoff + jq * 128 + 128],
                                    psF[:]))(qoff, jq),
                                deps=[t2])
                            fw_ops[(natt, jq)] = fw
            for gI in range(2, _ATTN_GROUPS[0]):
                first_ld[gI].deps.append(att_ops[8 * (gI - 2) + 8])
            P.observe(fw_ops[(8 * _ATTN_GROUPS[0], 3)])
            attn_last = fw_ops[(8 * _ATTN_GROUPS[0], 3)]

    ph2.close()
    ph23 = contextlib.ExitStack()
    y_fm = ph23.enter_context(nc.sbuf_tensor("y_fm", [128, NDC * TT], BF16))
    wo_sb = [ph23.enter_context(nc.sbuf_tensor(f"wo_sb{i}", [128, D], BF16)) for i in range(2)]
    psY = [ph23.enter_context(nc.psum_tensor(f"psY{i}", [128, 512], F32)) for i in range(2)]

    ld_wo = {}
    wo_mm = {}
    yev = {}
    gidx = 0
    for oc in range(NDC):
        wdeps = [mm_ops[NG - 1], exch_done, rope_ops[79], attn_last]
        if _DEBUG["dump"]:
            wdeps += [d_xfm, d_trig]
        if oc >= 2:
            wdeps.append(wo_mm[2 * (oc - 1)])
        ld_wo[oc] = P.add(
            "gpsimd",
            (lambda oc: lambda g: g.dma_start(out=wo_sb[oc % 2][:], in_=wo_n[oc]))(oc),
            deps=wdeps, dma=True)
        for th in range(2):
            gidx += 1
            mdeps = [ld_wo[oc], attn_last]
            if gidx >= 3:
                mdeps.append(yev[gidx - 2])

            def f_wo(oc=oc, th=th, bI=(gidx - 1) % 2):
                def fn(te):
                    return [te.matmul(
                        psY[bI][:], wo_sb[oc % 2][:, fc * 128:(fc + 1) * 128],
                        q_attn[:, fc * TT + th * 512: fc * TT + th * 512 + 512],
                        start=(fc == 0), stop=(fc == NDC - 1)) for fc in range(NDC)]
                return fn
            wo_mm[gidx] = P.add("tensor", f_wo(), deps=mdeps)
            yev[gidx] = P.add(
                "scalar",
                (lambda oc, th, bI: lambda s: s.copy(
                    y_fm[:, oc * TT + th * 512: oc * TT + th * 512 + 512],
                    psY[bI][:]))(oc, th, (gidx - 1) % 2),
                deps=[wo_mm[gidx]])
    y_done = yev[gidx]

    ph3 = contextlib.ExitStack()
    x1acc = ph3.enter_context(nc.sbuf_tensor("x1acc", [128, D], F32))
    xcent = ph3.enter_context(nc.sbuf_tensor("xcent", [128, D], F32))
    xres = ph3.enter_context(nc.sbuf_tensor("xres", [128, D], F32))
    lncol = ph3.enter_context(nc.sbuf_tensor("lncol", [128, 6], F32))
    psX1 = [ph3.enter_context(nc.psum_tensor(f"psX1_{i}", [128, 128], BF16)) for i in range(2)]

    ln_fin = {}
    spill3 = {}
    x1tev = {}
    nt3 = 0
    for tt in range(8):
        xdeps = [attn_last] if tt == 0 else [ln_fin[tt - 1]]
        xr = P.add("sync",
                   (lambda tt: lambda sy: sy.dma_start(
                       out=xres[:], in_=x_tok[tt * 128:(tt + 1) * 128, :]))(tt),
                   deps=xdeps, dma=True)
        ev_prev = None
        for oc in range(NDC):
            nt3 += 1
            tdeps = [y_done]
            if nt3 >= 3:
                tdeps.append(x1tev[nt3 - 2])
            if tt >= 1 and oc == 0:
                tdeps.append(spill3[tt - 1])
            t3 = P.add("tensor",
                       (lambda oc, tt, bI: lambda te: te.transpose(
                           psX1[bI][:],
                           y_fm[:, oc * TT + tt * 128: oc * TT + tt * 128 + 128],
                           ident[:]))(oc, tt, (nt3 - 1) % 2),
                       deps=tdeps)
            edeps = [t3, xr]
            if tt >= 1 and oc == 0:
                edeps.append(spill3[tt - 1])
            x1tev[nt3] = P.add(
                "vector",
                (lambda oc, bI: lambda v: v.tensor_tensor(
                    x1acc[:, oc * 128:(oc + 1) * 128], psX1[bI][:],
                    xres[:, oc * 128:(oc + 1) * 128], ALU.add))(oc, (nt3 - 1) % 2),
                deps=edeps)
        ln1 = P.add(
            "vector",
            lambda v: v.reduce_sum(lncol[:, 0:1], x1acc[:], AX.X),
            deps=[x1tev[nt3]])
        ln1b = P.add(
            "scalar",
            lambda s: s.activation(lncol[:, 4:5], lncol[:, 0:1], AF.Copy,
                                   scale=-1.0 / D),
            deps=[ln1])
        ln2 = P.add(
            "vector",
            lambda v: v.tensor_scalar(xcent[:], x1acc[:], lncol[:, 4:5], None,
                                      ALU.add),
            deps=[ln1b])
        ln3 = P.add(
            "scalar",
            lambda s: s.activation(x1acc[:], xcent[:], AF.Square,
                                   accum_out=lncol[:, 1:2]),
            deps=[ln2])
        ln3b = P.add(
            "vector",
            lambda v: v.tensor_copy(scr2[:, 6:7], beps[:]),
            deps=[ln3])
        ln4 = P.add(
            "scalar",
            lambda s: s.activation(lncol[:, 2:3], lncol[:, 1:2], AF.Sqrt,
                                   bias=beps[:], scale=1.0 / D),
            deps=[ln3b])
        ln4b = P.add(
            "vector",
            lambda v: v.reciprocal(lncol[:, 3:4], lncol[:, 2:3]),
            deps=[ln4])
        ln4c = P.add(
            "scalar",
            lambda s: s.activation(scr2[:, 6:7], beps[:], AF.Copy),
            deps=[ln4b])
        ln5 = P.add(
            "vector",
            lambda v: v.tensor_scalar(x1acc[:], xcent[:], lncol[:, 3:4], None,
                                      ALU.mult),
            deps=[ln4c])
        ln_fin[tt] = ln5
        spill3[tt] = P.add(
            "scalar",
            (lambda tt: lambda s: s.dma_start(
                out=x1_dram[tt * 128:(tt + 1) * 128, :], in_=x1acc[:]))(tt),
            deps=[ln5], dma=True)

    ph3.close()
    ph23.close()
    ph12.close()
    ph4 = contextlib.ExitStack()
    x1_fm = ph4.enter_context(nc.sbuf_tensor("x1_fm", [128, NDC * TT], BF16))
    wtb = [ph4.enter_context(nc.sbuf_tensor(f"wtb{i}", [128, NDC * 512], BF16)) for i in range(2)]
    ball = ph4.enter_context(nc.sbuf_tensor("ball", [128, D], BF16))
    at_sb = ph4.enter_context(nc.sbuf_tensor("at_sb", [128, NDC * 128], BF16))
    rwt_sb = ph4.enter_context(nc.sbuf_tensor("rwt_sb", [128, NDC * 8], BF16))
    bias_sb = ph4.enter_context(nc.sbuf_tensor("bias_sb", [1, D], BF16))
    rb_sb = ph4.enter_context(nc.sbuf_tensor("rb_sb", [1, 8], F32))
    u_fm = ph4.enter_context(nc.sbuf_tensor("u_fm", [128, TT], BF16))
    x1rd = ph4.enter_context(nc.sbuf_tensor("x1rd", [128, D], F32))
    x1bf = ph4.enter_context(nc.sbuf_tensor("x1bf", [128, D], BF16))
    qacc = ph4.enter_context(nc.sbuf_tensor("qacc", [128, 2 * NDC], F32))
    qbf = ph4.enter_context(nc.sbuf_tensor("qbf", [128, NDC], BF16))
    lgp = ph4.enter_context(nc.sbuf_tensor("lgp", [128, 8], F32))
    rtg = ph4.enter_context(nc.sbuf_tensor("rtg", [1, 8], F32))
    rtgb = ph4.enter_context(nc.sbuf_tensor("rtgb", [1, 8], BF16))
    rtT = ph4.enter_context(nc.sbuf_tensor("rtT", [8, 1], BF16))
    rexp = ph4.enter_context(nc.sbuf_tensor("rexp", [128, 1], F32))
    x1sl = [ph4.enter_context(nc.sbuf_tensor(f"x1sl{i}", [128, 512], F32)) for i in range(2)]
    o2ev = [ph4.enter_context(nc.sbuf_tensor(f"o2ev{i}", [128, 512], F32)) for i in range(2)]
    psF2 = [ph4.enter_context(nc.psum_tensor(f"psF2_{i}", [128, 128], BF16)) for i in range(2)]
    psR = ph4.enter_context(nc.psum_tensor("psR", [128, 32], F32))
    psL = ph4.enter_context(nc.psum_tensor("psL", [1, 8], F32))
    psRx = ph4.enter_context(nc.psum_tensor("psRx", [128, 1], BF16))
    psTA = ph4.enter_context(nc.psum_tensor("psTA", [128, 512], F32))
    psFF = [ph4.enter_context(nc.psum_tensor(f"psFF{i}", [128, 512], F32)) for i in range(2)]

    ld4 = {}
    for name, dst, src in (("at", at_sb, a_t2), ("ball", ball, ball_in),
                           ("rwt", rwt_sb, rwt2), ("bias", bias_sb, bias_in),
                           ("rb", rb_sb, rb_in)):
        ld4[name] = P.add("gpsimd",
                          (lambda dst, src: lambda g: g.dma_start(out=dst[:], in_=src))(dst, src),
                          deps=[exch_done] + list(spill3.values()), dma=True)

    x1c = {}
    rmm = {}
    qacc_add = {}
    t4e = {}
    nT4 = 0
    zop = P.add("vector", lambda v: v.memset(qacc[:, 0:NDC], 0.0),
                deps=list(spill3.values()))
    for tt in range(8):
        rdeps = list(spill3.values())
        if tt >= 1:
            rdeps.append(x1c[tt - 1])
        rd = P.add("sync",
                   (lambda tt: lambda sy: sy.dma_start(
                       out=x1rd[:], in_=x1_dram[tt * 128:(tt + 1) * 128, :]))(tt),
                   deps=rdeps, dma=True)
        cdeps = [rd]
        if tt >= 1:
            cdeps.append(rmm[tt - 1])
        x1c[tt] = P.add("vector", lambda v: v.tensor_copy(x1bf[:], x1rd[:]), deps=cdeps)
        for dc in range(NDC):
            nT4 += 1
            tdeps = [x1c[tt]]
            if nT4 >= 3:
                tdeps.append(t4e[nT4 - 2])
            t4 = P.add("tensor",
                       (lambda dc, bI: lambda te: te.transpose(
                           psF2[bI][:], x1bf[:, dc * 128:(dc + 1) * 128], ident[:]))(
                               dc, (nT4 - 1) % 2),
                       deps=tdeps)
            t4e[nT4] = P.add(
                "scalar",
                (lambda dc, tt, bI: lambda s: s.copy(
                    x1_fm[:, dc * TT + tt * 128: dc * TT + tt * 128 + 128], psF2[bI][:]))(
                        dc, tt, (nT4 - 1) % 2),
                deps=[t4])
        rdeps2 = [x1c[tt], c_vec, t4e[nT4]]
        if tt >= 1:
            rdeps2.append(qacc_add[tt - 1])

        def f_rmm(tt=tt):
            def fn(te):
                return [te.matmul(psR[:, dc:dc + 1], x1bf[:, dc * 128:(dc + 1) * 128],
                                  qm8[:, tt:tt + 1], start=True, stop=True)
                        for dc in range(NDC)]
            return fn
        rmm[tt] = P.add("tensor", f_rmm(), deps=rdeps2)
        qacc_add[tt] = P.add(
            "vector",
            (lambda tt: lambda v: v.tensor_tensor(
                qacc[:, NDC * ((tt + 1) % 2):NDC * ((tt + 1) % 2) + NDC],
                qacc[:, NDC * (tt % 2):NDC * (tt % 2) + NDC], psR[:], ALU.add))(tt),
            deps=[rmm[tt], zop])

    qbf_b = P.add("scalar", lambda s: s.activation(
        scr2[:, 5:6], beps[:], AF.Copy), deps=[qacc_add[7]])
    qbf_op = P.add("vector", lambda v: v.tensor_copy(qbf[:], qacc[:, 0:NDC]), deps=[qbf_b])
    lg_mm = P.add(
        "tensor",
        lambda te: [te.matmul(psL[:], qbf[:, dc:dc + 1], rwt_sb[:, dc * 8:(dc + 1) * 8],
                              start=(dc == 0), stop=(dc == NDC - 1)) for dc in range(NDC)],
        deps=[qbf_op, ld4["rwt"]])
    lgz = P.add("vector", lambda v: v.memset(lgp[:], 0.0), deps=list(spill3.values()))
    lg_ev = P.add("scalar", lambda s: s.copy(lgp[0:1, :], psL[:]), deps=[lg_mm, lgz])
    P.observe(lg_ev)

    def f_rbc(g):
        nm, val = lg_ev.sem
        g.wait_ge(sems[nm], val)
        r = g.partition_id()
        for k in g.Switch(r, 8):
            g.remote_dma_broadcast(rsl[:, 8 * k:8 * k + 8], lgp[:],
                                   sems["rsem"], sems["lsem"],
                                   rdests=[(0, d) for d in range(8)]).then_inc(sems["prep"], 1)
        g.wait_ge(sems["prep"], 2 * NWAVE + 1)
        g.trigger_dma(count=1)
        g.wait_ge(sems["rsem"], 16)
        return g.engine_nop()
    rbc = P.add("gpsimd", f_rbc, deps=[])
    P.observe(rbc)

    with P.switch(8, ["vector"], 8) as body:
        rsum_done = None
        for k8 in range(8):
            body(k8)
            bb = k8 // 4
            rsA = P.add(
                "vector",
                (lambda bb: lambda v: [
                    v.tensor_tensor(rtg2[:], rsl[0:1, 32 * bb:32 * bb + 8],
                                    rsl[0:1, 32 * bb + 8:32 * bb + 16], ALU.add),
                    v.tensor_tensor(rtg[:], rsl[0:1, 32 * bb + 16:32 * bb + 24],
                                    rsl[0:1, 32 * bb + 24:32 * bb + 32], ALU.add)])(bb),
                deps=[rbc])
            rsB = P.add("scalar", lambda s: s.activation(
                scr2[:, 5:6], beps[:], AF.Copy), deps=[rsA])
            rs_op = P.add(
                "vector",
                lambda v: v.tensor_tensor(lgp[0:1, 0:8], rtg2[:], rtg[:], ALU.add),
                deps=[rsB])
            P.observe(rs_op)
            if rsum_done is None:
                rsum_done = rs_op

    sm1 = P.add(
        "scalar",
        lambda s: s.activation(rtg2[:], lgp[0:1, 0:8], AF.Copy,
                               scale=1.0 / float(QE - QS)),
        deps=[rsum_done])
    sm1b = P.add(
        "vector",
        lambda v: v.tensor_tensor(rtg[:], rtg2[:], rb_sb[:], ALU.add),
        deps=[sm1, ld4["rb"]])
    sm1c = P.add("scalar", lambda s: s.activation(
        scr2[:, 5:6], beps[:], AF.Copy), deps=[sm1b])
    sm1d = P.add(
        "vector",
        lambda v: v.reduce_max(scr[0:1, 0:1], rtg[:], AX.X, negate=True),
        deps=[sm1c])
    sm2 = P.add("scalar",
                lambda s: s.activation(rtg2[:], rtg[:], AF.Exp, bias=scr[0:1, 0:1], scale=1.0),
                deps=[sm1d])
    sm2b = P.add(
        "vector",
        lambda v: v.reduce_sum(scr[0:1, 1:2], rtg2[:], AX.X),
        deps=[sm2])
    sm2b2 = P.add("scalar", lambda s: s.activation(
        scr2[:, 5:6], beps[:], AF.Copy), deps=[sm2b])
    sm2c = P.add(
        "vector",
        lambda v: v.reciprocal(scr[0:1, 2:3], scr[0:1, 1:2]),
        deps=[sm2b2])
    sm2d = P.add("scalar", lambda s: s.activation(
        scr2[:, 5:6], beps[:], AF.Copy), deps=[sm2c])
    sm3 = P.add(
        "vector",
        lambda v: [v.tensor_scalar(rtgb[:], rtg2[:], scr[0:1, 2:3], None, ALU.mult),
                   v.tensor_scalar(rtg[:], rtg2[:], scr[0:1, 2:3], None, ALU.mult)],
        deps=[sm2d])
    rx1 = P.add("tensor",
                lambda te: te.transpose(psRx[0:8, 0:1], rtgb[:], ident[0:1, 0:1]),
                deps=[sm3])
    rx2 = P.add("scalar", lambda s: s.copy(rtT[:], psRx[0:8, 0:1]), deps=[rx1])
    rx3 = P.add("tensor",
                lambda te: te.matmul(psFF[0][:, 0:1], selm[:], rtT[:], start=True, stop=True),
                deps=[rx2])
    rx4 = P.add("scalar", lambda s: s.mul(rexp[:], psFF[0][:, 0:1], 2.0), deps=[rx3])

    u_op = {}
    for th in range(2):
        tdeps = [t4e[nT4], ld4["at"]]
        if th == 1:
            tdeps.append(u_op[0])

        def f_ta(th=th):
            def fn(te):
                return [te.matmul(psTA[:], at_sb[:, dc * 128:(dc + 1) * 128],
                                  x1_fm[:, dc * TT + th * 512: dc * TT + th * 512 + 512],
                                  start=(dc == 0), stop=(dc == NDC - 1)) for dc in range(NDC)]
            return fn
        ta = P.add("tensor", f_ta(), deps=tdeps)
        u_op[th] = P.add(
            "vector",
            (lambda th: lambda v: v.tensor_scalar(
                u_fm[:, th * 512:(th + 1) * 512], psTA[:], rexp[:], None, ALU.mult))(th),
            deps=[ta, rx4])

    ld_wt = {}
    ffn_mm = {}
    o2e = {}
    o2s = {}
    idx4 = 0
    for osl in range(8):
        wdeps = [exch_done] + list(spill3.values())
        if osl >= 2:
            wdeps.append(ffn_mm[(osl - 2, 7)])
        ld_wt[osl] = P.add(
            "gpsimd",
            (lambda osl: lambda g: g.dma_start(out=wtb[osl % 2][:], in_=wt_l[osl]))(osl),
            deps=wdeps, dma=True)
        for tt in range(8):
            idx4 += 1
            ldeps = list(spill3.values())
            if idx4 >= 3:
                ldeps.append(o2e[idx4 - 2])
            xl = P.add(
                "sync",
                (lambda tt, osl, bI: lambda sy: sy.dma_start(
                    out=x1sl[bI][:],
                    in_=x1_dram[tt * 128:(tt + 1) * 128, osl * 512:(osl + 1) * 512]))(
                        tt, osl, (idx4 - 1) % 2),
                deps=ldeps, dma=True)
            mdeps = [ld_wt[osl], u_op[1], t4e[nT4], ld4["ball"], ld4["bias"]]
            if idx4 >= 3:
                mdeps.append(o2e[idx4 - 2])

            def f_ffn(tt=tt, osl=osl, bI=(idx4 - 1) % 2):
                def fn(te):
                    r = [te.matmul(
                        psFF[bI][:], x1_fm[:, dc * TT + tt * 128: dc * TT + tt * 128 + 128],
                        wtb[osl % 2][:, dc * 512:(dc + 1) * 512],
                        start=(dc == 0), stop=False) for dc in range(NDC)]
                    r.append(te.matmul(psFF[bI][:], u_fm[:, tt * 128:(tt + 1) * 128],
                                       ball[:, osl * 512:(osl + 1) * 512],
                                       start=False, stop=False))
                    r.append(te.matmul(psFF[bI][:], onesrow[:],
                                       bias_sb[0:1, osl * 512:(osl + 1) * 512],
                                       start=False, stop=True))
                    return r
                return fn
            ffn_mm[(osl, tt)] = P.add("tensor", f_ffn(), deps=mdeps)
            o2e[idx4] = P.add(
                "vector",
                (lambda bI: lambda v: v.tensor_tensor(o2ev[bI][:], psFF[bI][:], x1sl[bI][:],
                                                      ALU.add))((idx4 - 1) % 2),
                deps=[ffn_mm[(osl, tt)], xl])
            o2s[(osl, tt)] = P.add(
                "scalar",
                (lambda tt, osl, bI: lambda s: s.dma_start(
                    out=o2_dram[tt * 128:(tt + 1) * 128, osl * 512:(osl + 1) * 512],
                    in_=o2ev[bI][:]))(tt, osl, (idx4 - 1) % 2),
                deps=[o2e[idx4]], dma=True)

    ph4.close()
    ph5 = contextlib.ExitStack()
    o2r = ph5.enter_context(nc.sbuf_tensor("o2r", [128, D], F32))
    oc5 = ph5.enter_context(nc.sbuf_tensor("oc5", [128, D], F32))
    osq = ph5.enter_context(nc.sbuf_tensor("osq", [128, D], F32))
    lnc2 = ph5.enter_context(nc.sbuf_tensor("lnc2", [128, 6], F32))

    out_ops = []
    prev_l3 = None
    prev_wr = None
    for tt in range(8):
        deps = list(o2s.values())
        if prev_l3 is not None:
            deps.append(prev_l3)
        rd5 = P.add("sync",
                    (lambda tt: lambda sy: sy.dma_start(
                        out=o2r[:], in_=o2_dram[tt * 128:(tt + 1) * 128, :]))(tt),
                    deps=deps, dma=True)
        l1d = [rd5]
        if prev_wr is not None:
            l1d.append(prev_wr)
        l1 = P.add(
            "vector",
            lambda v: v.reduce_sum(lnc2[:, 0:1], o2r[:], AX.X),
            deps=l1d)
        l1b = P.add(
            "scalar",
            lambda s: s.activation(lnc2[:, 4:5], lnc2[:, 0:1], AF.Copy,
                                   scale=-1.0 / D),
            deps=[l1])
        l2 = P.add(
            "vector",
            lambda v: v.tensor_scalar(oc5[:], o2r[:], lnc2[:, 4:5], None, ALU.add),
            deps=[l1b])
        l2b = P.add(
            "scalar",
            lambda s: s.activation(o2r[:], oc5[:], AF.Square,
                                   accum_out=lnc2[:, 1:2]),
            deps=[l2])
        l2c = P.add(
            "vector",
            lambda v: v.tensor_copy(scr2[:, 7:8], beps[:]),
            deps=[l2b])
        l2d = P.add(
            "scalar",
            lambda s: s.activation(lnc2[:, 2:3], lnc2[:, 1:2], AF.Sqrt,
                                   bias=beps[:], scale=1.0 / D),
            deps=[l2c])
        l2e = P.add(
            "vector",
            lambda v: v.reciprocal(lnc2[:, 3:4], lnc2[:, 2:3]),
            deps=[l2d])
        l2f = P.add(
            "scalar",
            lambda s: s.activation(scr2[:, 7:8], beps[:], AF.Copy),
            deps=[l2e])
        l3 = P.add(
            "vector",
            lambda v: v.tensor_scalar(osq[:], oc5[:], lnc2[:, 3:4], None, ALU.mult),
            deps=[l2f])
        wr = P.add("scalar",
                   (lambda tt: lambda s: s.dma_start(
                       out=out_h[tt * 128:(tt + 1) * 128, :], in_=osq[:]))(tt),
                   deps=[l3], dma=True)
        out_ops.append(wr)
        prev_l3 = l3
        prev_wr = wr

    if _DEBUG["dump"]:
        P.add("gpsimd", lambda g: g.dma_start(out=dbg["kall"], in_=k_all[:, :, :]),
              deps=[exch_done], dma=True)
        P.add("gpsimd", lambda g: g.dma_start(out=dbg["vall"], in_=v_all[:, :]),
              deps=[exch_done], dma=True)
        P.add("gpsimd", lambda g: g.dma_start(out=dbg["attn"], in_=q_attn[:]),
              deps=[attn_last], dma=True)
        P.add("gpsimd", lambda g: g.dma_start(out=dbg["x1"], in_=x1_dram[:, :]),
              deps=[spill3[7]], dma=True)
        P.add("gpsimd", lambda g: g.dma_start(out=dbg["rout"], in_=rtg[:]),
              deps=[sm3, out_ops[-1]], dma=True)

    P.add("gpsimd", lambda g: g.engine_nop(), deps=list(out_ops))

    if trunc is not None:
        kept = set()
        if trunc >= 1000:
            for seg in P.segments:
                d = seg["ops"] if seg["kind"] == "plain" else None
                if d is None:
                    for b in seg["bodies"]:
                        for e in ENGS:
                            b[e] = [o for o in b[e] if o.order < trunc - 1000]
                else:
                    for e in ENGS:
                        d[e] = [o for o in d[e] if o.order < trunc - 1000]
        else:
            P.segments = P.segments[:trunc]
        for seg in P.segments:
            dicts = [seg["ops"]] if seg["kind"] == "plain" else seg["bodies"]
            for dd in dicts:
                for e in ENGS:
                    for o in dd[e]:
                        kept.add(id(o))
        for seg in P.segments:
            dicts = [seg["ops"]] if seg["kind"] == "plain" else seg["bodies"]
            for dd in dicts:
                for e in ENGS:
                    for o in dd[e]:
                        o.deps = [dp for dp in o.deps if id(dp) in kept]
    P.schedule()
    P.emit(nc, blk, sems)
    ph5.close()
    st.close()
    return nc, dbg


def _shard_inputs(inputs):
    X = np.asarray(inputs["L0_attn_X"], np.float32)
    Wq = np.asarray(inputs["L0_attn_Wq"], np.float32)
    Wk = np.asarray(inputs["L0_attn_Wk"], np.float32)
    Wv = np.asarray(inputs["L0_attn_Wv"], np.float32)
    Wo = np.asarray(inputs["L0_attn_Wo"], np.float32)
    W = np.asarray(inputs["L0_ffn_W"], np.float32)
    b = np.asarray(inputs["L0_ffn_b"], np.float32)
    A = np.asarray(inputs["L0_ffn_A"], np.float32)
    B = np.asarray(inputs["L0_ffn_B"], np.float32)
    rW = np.asarray(inputs["L0_ffn_router_W"], np.float32)
    rb = np.asarray(inputs["L0_ffn_router_b"], np.float32)

    wq_l = np.ascontiguousarray(Wq.reshape(NDC, 128, NH, 128).transpose(2, 1, 0, 3)
                                ).reshape(NH, 128, D)
    wk_l = np.ascontiguousarray(Wk.reshape(NDC, 128, NKV, 128).transpose(2, 1, 0, 3)
                                ).reshape(NKV, 128, D)
    wv_l = np.ascontiguousarray(Wv.reshape(NDC, 128, NKV, 128).transpose(2, 1, 0, 3)
                                ).reshape(NKV, 128, D)
    wo_n = np.ascontiguousarray(Wo.reshape(NDC, 128, NDC, 128).transpose(2, 1, 0, 3)
                                ).reshape(NDC, 128, D)
    wt_l = np.ascontiguousarray(W.reshape(8, 512, NDC, 128).transpose(0, 3, 2, 1)
                                ).reshape(8, 128, NDC * 512)
    A_all = A.reshape(128, D)
    a_t2 = np.ascontiguousarray(A_all.T.reshape(NDC, 128, 128).transpose(1, 0, 2)
                                ).reshape(128, NDC * 128)
    ball = np.ascontiguousarray(B.transpose(0, 2, 1).reshape(128, D))
    rwt2 = np.ascontiguousarray(rW.T.reshape(NDC, 128, 8).transpose(1, 0, 2)
                                ).reshape(128, NDC * 8)
    bias_row = np.ascontiguousarray(b.reshape(1, D))
    rb_row = np.ascontiguousarray(rb.reshape(1, 8))

    in_maps = []
    for c in range(8):
        bb = c // 4
        p = c % 4
        rows = np.r_[512 * p:512 * (p + 1), 512 * (7 - p):512 * (8 - p)]
        xc = np.ascontiguousarray(X[bb][rows])
        xfm = np.ascontiguousarray(xc.T)
        pos = rows.astype(np.float32)
        pos_c = np.ascontiguousarray(pos.reshape(8, 128).T)
        posb = np.ascontiguousarray(np.broadcast_to(pos.reshape(1, TT), (128, TT)))
        in_maps.append({
            "x_tok": xc, "x_fmT": xfm,
            "wq_l": wq_l, "wk_l": wk_l, "wv_l": wv_l, "wo_n": wo_n,
            "wt_l": wt_l, "a_t2": a_t2, "ball_in": ball, "bias_in": bias_row,
            "rwt2": rwt2, "rb_in": rb_row, "pos_in": pos_c, "posb_in": posb,
        })
    return in_maps


def kernel(**inputs):
    in_maps = _shard_inputs(inputs)
    nc, dbg = build_program()
    nc.finalize()
    res = run_bass_kernel_spmd(nc, in_maps, core_ids=list(range(8)))
    out = np.zeros((2, T, D), np.float32)
    for c in range(8):
        bb = c // 4
        p = c % 4
        o = res.results[c]["out_h"]
        out[bb, 512 * p:512 * (p + 1)] = o[0:512]
        out[bb, 512 * (7 - p):512 * (8 - p)] = o[512:1024]
    if _DEBUG["dump"]:
        kernel._last_debug = res.results
    return out


# version 10
# speedup vs baseline: 1.2625x; 1.2625x over previous
import contextlib
import math
import numpy as np
import concourse.bacc as bacc
import concourse.bass as bass
import concourse.mybir as mybir
from concourse.bass_utils import run_bass_kernel_spmd

F32 = mybir.dt.float32
BF16 = mybir.dt.bfloat16
AF = mybir.ActivationFunctionType
ALU = mybir.AluOpType
AX = mybir.AxisListType

D = 4096
T = 4096
NH = 32
NKV = 8
TT = 1024
NDC = 32
SHIFT = 12.0
SCALE = 1.0 / math.sqrt(128.0)
EPS = 1e-5
QS, QE = 611, T - 1
PI = math.pi
TWO_PI = 2.0 * math.pi
NWAVE = 32
RING = 16
TWO_PI_C1 = 6.28125
TWO_PI_C2 = float(np.float32(2.0 * math.pi - 6.28125))
TWO_PI_C3 = float(2.0 * math.pi - 6.28125 - np.float32(2.0 * math.pi - 6.28125))


_DEBUG = {"dump": False}
_ATTN_GROUPS = [8]

ENGS = ("tensor", "scalar", "vector", "sync", "gpsimd")


_OPN = [0]


class Op:
    __slots__ = ("eng", "fn", "deps", "dma", "sem", "order")

    def __init__(self, eng, fn, deps, dma):
        self.eng = eng
        self.fn = fn
        assert all(d is not None for d in deps), "None dep (forward reference?)"
        self.deps = list(deps)
        self.dma = dma
        self.sem = None
        self.order = _OPN[0]
        _OPN[0] += 1


class Prog:
    def __init__(self):
        self.segments = []
        self.forced = set()
        self._cur = None
        self._open_plain()

    def _open_plain(self):
        self._cur = {e: [] for e in ENGS}
        self.segments.append({"kind": "plain", "ops": self._cur})

    def add(self, eng, fn, deps=(), dma=False):
        op = Op(eng, fn, deps, dma)
        self._cur[eng].append(op)
        return op

    def observe(self, op):
        self.forced.add(id(op))

    @contextlib.contextmanager
    def switch(self, n, engines, mod):
        seg = {"kind": "switch", "n": n, "engines": list(engines), "mod": mod,
               "bodies": [{e: [] for e in ENGS} for _ in range(n)]}
        self.segments.append(seg)
        saved = self._cur

        def body(k):
            self._cur = seg["bodies"][k]
        try:
            yield body
        finally:
            self._cur = saved
            self._open_plain()

    DMA_POOL = {"tensor": 2, "scalar": 6, "vector": 4, "sync": 8, "gpsimd": 8}

    def schedule(self):
        observed = set(self.forced)

        def mark(ops_dict):
            for e in ENGS:
                for op in ops_dict[e]:
                    if op.dma:
                        observed.add(id(op))
                    for d in op.deps:
                        if d.eng != op.eng or d.dma:
                            observed.add(id(d))
        for seg in self.segments:
            if seg["kind"] == "plain":
                mark(seg["ops"])
            else:
                for b in seg["bodies"]:
                    mark(b)

        cnt = {}
        for e in ENGS:
            cnt[e + "_c"] = 0
            for i in range(self.DMA_POOL[e]):
                cnt[f"{e}_d{i}"] = 0
            cnt[e + "_dn"] = 0

        def walk(ops_dict, counters):
            for e in ENGS:
                for op in ops_dict[e]:
                    if id(op) in observed:
                        if op.dma:
                            i = counters[e + "_dn"] % self.DMA_POOL[e]
                            counters[e + "_dn"] += 1
                            nm = f"{e}_d{i}"
                            counters[nm] += 16
                            op.sem = (nm, counters[nm])
                        else:
                            nm = e + "_c"
                            counters[nm] += 1
                            op.sem = (nm, counters[nm])

        for seg in self.segments:
            if seg["kind"] == "plain":
                walk(seg["ops"], cnt)
            else:
                finals = []
                for b in seg["bodies"]:
                    c2 = dict(cnt)
                    walk(b, c2)
                    finals.append(c2)
                for c2 in finals[1:]:
                    assert c2 == finals[0], (
                        f"non-uniform switch tick counts: {finals[0]} vs {c2}")
                cnt = finals[0]
        self.final = dict(cnt)

    def emit(self, nc, blk, sems):
        ENGOBJ = {"tensor": nc.tensor, "scalar": nc.scalar, "vector": nc.vector,
                  "sync": nc.sync, "gpsimd": nc.gpsimd}

        def emit_op(op):
            E = ENGOBJ[op.eng]
            waits = {}
            for d in op.deps:
                if d.eng == op.eng and not d.dma:
                    continue
                assert d.sem is not None, "dep never scheduled"
                nm, v = d.sem
                waits[nm] = max(waits.get(nm, 0), v)
            if op.sem is not None and op.dma:
                nm, v = op.sem
                if v > 16:
                    waits[nm] = max(waits.get(nm, 0), v - 16)
            for nm, v in waits.items():
                E.wait_ge(sems[nm], v)
            r = op.fn(E)
            if r is None:
                instrs = []
            elif isinstance(r, (list, tuple)):
                instrs = [i for i in r if i is not None]
            else:
                instrs = [r]
            if instrs:
                if op.sem is not None:
                    nm, _ = op.sem
                    instrs[-1].then_inc(sems[nm], 16 if op.dma else 1)
            else:
                assert op.sem is None

        def em(_):
            for seg in self.segments:
                if seg["kind"] == "plain":
                    for e in ENGS:
                        for op in seg["ops"][e]:
                            emit_op(op)
                else:
                    engs = [e for e in ENGS
                            if any(seg["bodies"][k][e] for k in range(seg["n"]))]
                    if not engs:
                        continue
                    eobjs = [ENGOBJ[e] for e in engs]
                    idx = [E.partition_id() % seg["mod"] for E in eobjs]
                    for k in nc.Switch(engines=eobjs, index=idx, n=seg["n"]):
                        for e in engs:
                            for op in seg["bodies"][k][e]:
                                emit_op(op)
        blk.gpsimd(em)


def build_program(trunc=None):
    nc = bacc.Bacc("TRN2", target_bir_lowering=False, num_devices=8,
                   detect_race_conditions=False)
    st = contextlib.ExitStack()

    x_tok = nc.dram_tensor("x_tok", [TT, D], F32, kind="ExternalInput").ap()
    x_fmT = nc.dram_tensor("x_fmT", [D, TT], F32, kind="ExternalInput").ap()
    wq_l = nc.dram_tensor("wq_l", [NH, 128, D], F32, kind="ExternalInput").ap()
    wk_l = nc.dram_tensor("wk_l", [NKV, 128, D], F32, kind="ExternalInput").ap()
    wv_l = nc.dram_tensor("wv_l", [NKV, 128, D], F32, kind="ExternalInput").ap()
    wo_n = nc.dram_tensor("wo_n", [NDC, 128, D], F32, kind="ExternalInput").ap()
    wt_l = nc.dram_tensor("wt_l", [8, 128, NDC * 512], F32, kind="ExternalInput").ap()
    a_t2 = nc.dram_tensor("a_t2", [128, NDC * 128], F32, kind="ExternalInput").ap()
    ball_in = nc.dram_tensor("ball_in", [128, D], F32, kind="ExternalInput").ap()
    bias_in = nc.dram_tensor("bias_in", [1, D], F32, kind="ExternalInput").ap()
    rwt2 = nc.dram_tensor("rwt2", [128, NDC * 8], F32, kind="ExternalInput").ap()
    rb_in = nc.dram_tensor("rb_in", [1, 8], F32, kind="ExternalInput").ap()
    pos_in = nc.dram_tensor("pos_in", [128, 8], F32, kind="ExternalInput").ap()
    posb_in = nc.dram_tensor("posb_in", [128, TT], F32, kind="ExternalInput").ap()
    out_h = nc.dram_tensor("out_h", [TT, D], F32, kind="ExternalOutput").ap()

    k_all = nc.dram_tensor("k_all", [NKV, 128, T], BF16, kind="Internal").ap()
    v_all = nc.dram_tensor("v_all", [T, 1024], BF16, kind="Internal").ap()
    x1_dram = nc.dram_tensor("x1_dram", [TT, D], F32, kind="Internal").ap()
    o2_dram = nc.dram_tensor("o2_dram", [TT, D], F32, kind="Internal").ap()

    dbg = {}
    if _DEBUG["dump"]:
        dbg["q"] = nc.dram_tensor("dbg_q", [128, NH * TT], F32, kind="ExternalOutput").ap()
        dbg["kall"] = nc.dram_tensor("dbg_kall", [NKV, 128, T], F32, kind="ExternalOutput").ap()
        dbg["vall"] = nc.dram_tensor("dbg_vall", [T, 1024], F32, kind="ExternalOutput").ap()
        dbg["attn"] = nc.dram_tensor("dbg_attn", [128, NH * TT], F32, kind="ExternalOutput").ap()
        dbg["x1"] = nc.dram_tensor("dbg_x1", [TT, D], F32, kind="ExternalOutput").ap()
        dbg["rout"] = nc.dram_tensor("dbg_rout", [1, 8], F32, kind="ExternalOutput").ap()
        dbg["xfm"] = nc.dram_tensor("dbg_xfm", [128, NDC * TT], F32, kind="ExternalOutput").ap()
        dbg["trig"] = nc.dram_tensor("dbg_trig", [128, 2 * TT], F32, kind="ExternalOutput").ap()
        dbg["ring"] = nc.dram_tensor("dbg_ring", [128, RING * 512], F32, kind="ExternalOutput").ap()
        dbg["scr"] = nc.dram_tensor("dbg_scr", [128, 8], F32, kind="ExternalOutput").ap()
        dbg["tmpA"] = nc.dram_tensor("dbg_tmpA", [128, TT], F32, kind="ExternalOutput").ap()

    sems = {}
    for e in ENGS:
        sems[e + "_c"] = st.enter_context(nc.semaphore(e + "_c"))
        for i in range(Prog.DMA_POOL[e]):
            sems[f"{e}_d{i}"] = st.enter_context(nc.semaphore(f"{e}_d{i}"))
    for n in ("prep", "dsem", "asem", "lsem", "rsem", "outd"):
        sems[n] = st.enter_context(nc.semaphore(n))

    blk = st.enter_context(nc.Block())
    P = Prog()

    ident = st.enter_context(nc.sbuf_tensor("ident", [128, 128], BF16))
    identf = st.enter_context(nc.sbuf_tensor("identf", [1, 1], F32))
    rperm = st.enter_context(nc.sbuf_tensor("rperm", [128, 128], BF16))
    selm = st.enter_context(nc.sbuf_tensor("selm", [8, 128], BF16))
    onescol = st.enter_context(nc.sbuf_tensor("onescol", [128, 1], BF16))
    onesrow = st.enter_context(nc.sbuf_tensor("onesrow", [1, 128], BF16))
    b12 = st.enter_context(nc.sbuf_tensor("b12", [128, 1], F32))
    beps = st.enter_context(nc.sbuf_tensor("beps", [128, 1], F32))
    pos8 = st.enter_context(nc.sbuf_tensor("pos8", [128, 8], F32))
    qm8 = st.enter_context(nc.sbuf_tensor("qm8", [128, 8], BF16))
    scr = st.enter_context(nc.sbuf_tensor("scr", [128, 8], F32))
    scr2 = st.enter_context(nc.sbuf_tensor("scr2", [128, 8], F32))
    rtg2 = st.enter_context(nc.sbuf_tensor("rtg2", [1, 8], F32))
    recv = st.enter_context(nc.sbuf_tensor("recv", [128, 8 * 512], BF16))
    ackt = st.enter_context(nc.sbuf_tensor("ackt", [128, 8], F32))
    ackr = st.enter_context(nc.sbuf_tensor("ackr", [128, 8], F32))
    rsl = st.enter_context(nc.sbuf_tensor("rsl", [128, 64], F32))

    ph12 = contextlib.ExitStack()
    mask = ph12.enter_context(nc.sbuf_tensor("mask", [128, 1024], BF16))
    q_attn = ph12.enter_context(nc.sbuf_tensor("q_attn", [128, NH * TT], BF16))

    ph1 = contextlib.ExitStack()
    trig1 = ph1.enter_context(nc.sbuf_tensor("trig1", [128, TT], F32))
    trig2 = ph1.enter_context(nc.sbuf_tensor("trig2", [128, TT], BF16))
    tmpA = ph1.enter_context(nc.sbuf_tensor("tmpA", [128, TT], F32))
    ph0 = contextlib.ExitStack()
    kA = ph0.enter_context(nc.sbuf_tensor("kA", [128, TT], F32))
    kB = ph0.enter_context(nc.sbuf_tensor("kB", [128, TT], F32))
    ph0.close()
    x_fm = ph1.enter_context(nc.sbuf_tensor("x_fm", [128, NDC * TT], BF16))
    w_sb = [ph1.enter_context(nc.sbuf_tensor(f"w_sb{i}", [128, D], BF16)) for i in range(3)]
    ring = ph1.enter_context(nc.sbuf_tensor("ring", [128, RING * 512], BF16))
    praw = [ph1.enter_context(nc.sbuf_tensor(f"praw{i}", [128, 512], BF16)) for i in range(2)]
    vfm_all = ph1.enter_context(nc.sbuf_tensor("vfm_all", [128, 8 * 1024], BF16))
    ps1 = [ph1.enter_context(nc.psum_tensor(f"ps1_{i}", [128, 512], F32)) for i in range(2)]
    psr = [ph1.enter_context(nc.psum_tensor(f"psr_{i}", [128, 512], F32)) for i in range(2)]
    pst1 = [ph1.enter_context(nc.psum_tensor(f"pst1_{i}", [128, 128], BF16)) for i in range(2)]

    c_pos8 = P.add("gpsimd", lambda g: g.dma_start(out=pos8[:], in_=pos_in), dma=True)
    c_posb = P.add("gpsimd", lambda g: g.dma_start(out=tmpA[:], in_=posb_in), dma=True)
    c_iota = P.add("gpsimd", lambda g: [
        g.iota(ident[:], pattern=[[1, 128]], base=0, channel_multiplier=-1,
               allow_small_or_imprecise_dtypes=True),
        g.iota(rperm[:], pattern=[[-1, 128]], base=-64, channel_multiplier=1,
               allow_small_or_imprecise_dtypes=True),
        g.iota(selm[:], pattern=[[1, 128]], base=0, channel_multiplier=-16,
               allow_small_or_imprecise_dtypes=True),
        g.iota(mask[:], pattern=[[1, 1024]], base=-512, channel_multiplier=-1,
               allow_small_or_imprecise_dtypes=True),
        g.iota(trig1[:, 0:1], pattern=[[0, 1]], base=0, channel_multiplier=1,
               allow_small_or_imprecise_dtypes=True),
        g.iota(trig1[:, 1:2], pattern=[[0, 1]], base=0, channel_multiplier=1,
               allow_small_or_imprecise_dtypes=True),
    ])

    def f_consts(v):
        return [
            v.tensor_single_scalar(ident[:], ident[:], 0.0, ALU.is_equal),
            v.memset(identf[:], 1.0),
            v.tensor_single_scalar(kA[:, 0:128], rperm[:], -128.0, ALU.is_equal),
            v.tensor_single_scalar(kA[:, 128:256], rperm[:], 0.0, ALU.is_equal),
            v.tensor_tensor(rperm[:], kA[:, 0:128], kA[:, 128:256], ALU.add),
            v.tensor_single_scalar(kA[0:8, 0:128], selm[:], 0.0, ALU.is_ge),
            v.tensor_single_scalar(kA[0:8, 128:256], selm[:], 15.0, ALU.is_le),
            v.tensor_tensor(selm[:], kA[0:8, 0:128], kA[0:8, 128:256], ALU.mult),
            v.tensor_single_scalar(kA[:], mask[:], 0.0, ALU.is_ge),
            v.tensor_scalar(mask[:], kA[:, 0:1024], 1.0, 1.0e30, ALU.subtract, ALU.mult),
            v.memset(onescol[:], 1.0),
            v.memset(ackt[:], 0.0),
            v.memset(rsl[:], 0.0),
            v.memset(onesrow[:], 1.0),
            v.memset(b12[:], -SHIFT),
            v.memset(beps[:], EPS),
            v.tensor_single_scalar(rsl[:, 8:16], pos8[:], float(QS), ALU.is_ge),
            v.tensor_single_scalar(scr2[:], pos8[:], float(QE - 1), ALU.is_le),
            v.tensor_scalar(scr[:, 4:5], trig1[:, 0:1], 63.5,
                            math.log(10000.0), ALU.is_ge, ALU.mult),
            v.tensor_scalar(scr[:, 6:7], trig1[:, 1:2], 63.5, 2.0,
                            ALU.is_ge, ALU.mult),
        ]
    c_vec = P.add("vector", f_consts, deps=[c_pos8, c_iota])
    c_if = P.add("scalar", lambda s: s.activation(
        scr[:, 3:4], trig1[:, 0:1], AF.Exp, bias=scr[:, 4:5],
        scale=-math.log(10000.0) / 64.0),
        deps=[c_vec])
    c_qm8 = P.add("vector", lambda v: v.tensor_tensor(
        qm8[:], rsl[:, 8:16], scr2[:], ALU.mult), deps=[c_if])

    RED_BITS = [512, 256, 128, 64, 32, 16, 8, 4, 2, 1]

    def red_steps(v, cur, oth, w):
        r = []
        for i, b in enumerate(RED_BITS):
            thr = float(np.float32(b * TWO_PI))
            a, o = (cur, oth) if i % 2 == 0 else (oth, cur)
            r.append(v.tensor_scalar(w[:], a[:], thr, thr, ALU.is_ge, ALU.mult))
            r.append(v.tensor_tensor(o[:], a[:], w[:], ALU.subtract))
        r.append(v.tensor_scalar(w[:], cur[:], float(PI),
                                 float(np.float32(TWO_PI)), ALU.is_ge, ALU.mult))
        r.append(v.tensor_tensor(oth[:], cur[:], w[:], ALU.subtract))
        return r

    def f_ang(v):
        return [
            v.tensor_scalar(kB[:], tmpA[:], scr[:, 3:4], None, ALU.mult),
            v.tensor_single_scalar(trig1[:], kB[:], PI / 2.0, ALU.add),
        ] + red_steps(v, trig1, tmpA, kA)
    c_ang = P.add("vector", f_ang, deps=[c_if, c_posb])
    c_sin1 = P.add("scalar", lambda s: s.activation(trig1[:], tmpA[:], AF.Sin),
                   deps=[c_ang])
    c_ang2 = P.add("vector", lambda v: red_steps(v, kB, tmpA, kA),
                   deps=[c_sin1])
    c_sin2 = P.add("scalar", lambda s: s.activation(kA[:], tmpA[:], AF.Sin),
                   deps=[c_ang2])
    c_trig = P.add("vector", lambda v: [
        v.tensor_scalar(tmpA[:], kA[:], scr[:, 6:7], None, ALU.mult),
        v.tensor_tensor(trig2[:], tmpA[:], kA[:], ALU.subtract)], deps=[c_sin2])

    plan = ([("wk", kv, th) for kv in range(NKV) for th in range(2)] +
            [("wv", c8, th) for c8 in range(NKV) for th in range(2)] +
            [("wq", h, th) for h in range(NH) for th in range(2)])
    NG = len(plan)
    rope_gis = [gi for gi, pl in enumerate(plan) if pl[0] in ("wk", "wq")]

    ld_xfm = []
    for dc in range(NDC):
        ld_xfm.append(P.add(
            "gpsimd",
            (lambda dc: lambda g: g.dma_start(
                out=x_fm[:, dc * TT:(dc + 1) * TT],
                in_=x_fmT[dc * 128:(dc + 1) * 128, :]))(dc),
            deps=[c_trig], dma=True))

    slabs = ([("wk", wk_l, kv) for kv in range(NKV)] +
             [("wv", wv_l, c8) for c8 in range(NKV)] +
             [("wq", wq_l, h) for h in range(NH)])
    mm_ops = [None] * NG
    ev_ops = [None] * NG
    rope_ops = {}
    piece_ops = [None] * NWAVE
    q_ready = {}
    ld_w = []
    for i, (kind, src, idx) in enumerate(slabs):
        deps = [c_trig]
        ld_w.append(P.add(
            "gpsimd",
            (lambda src, idx, bI: lambda g: g.dma_start(out=w_sb[bI][:], in_=src[idx]))(
                src, idx, i % 3),
            deps=deps, dma=True))

    for gi, (kind, idx, th) in enumerate(plan):
        slab = gi // 2
        deps = [ld_w[slab]] + ld_xfm + ([ev_ops[gi - 2]] if gi >= 2 else [])

        def f_mm(gi=gi, slab=slab, th=th):
            def fn(te):
                ps = ps1[gi % 2]
                return [te.matmul(ps[:], w_sb[slab % 3][:, dc * 128:(dc + 1) * 128],
                                  x_fm[:, dc * TT + th * 512: dc * TT + th * 512 + 512],
                                  start=(dc == 0), stop=(dc == NDC - 1))
                        for dc in range(NDC)]
            return fn
        mm_ops[gi] = P.add("tensor", f_mm(), deps=deps)

        if kind == "wv":
            ev_ops[gi] = P.add(
                "scalar",
                (lambda gi, idx, th: lambda s: s.copy(
                    vfm_all[:, idx * 1024 + th * 512: idx * 1024 + th * 512 + 512],
                    ps1[gi % 2][:]))(gi, idx, th),
                deps=[mm_ops[gi]])
        else:
            r = rope_gis.index(gi)
            deps_e = [mm_ops[gi]] + ([rope_ops[r - 2]] if r >= 2 else [])
            ev_ops[gi] = P.add(
                "scalar",
                (lambda gi, r: lambda s: s.copy(praw[r % 2][:], ps1[gi % 2][:]))(gi, r),
                deps=deps_e)
            deps_r = [ev_ops[gi]] + ([rope_ops[r - 2]] if r >= 2 else [])
            rot = P.add(
                "tensor",
                (lambda r: lambda te: te.matmul(psr[r % 2][:], rperm[:], praw[r % 2][:],
                                                start=True, stop=True))(r),
                deps=deps_r)
            if kind == "wk":
                dst = ring[:, (2 * idx + th) % RING * 512:
                           ((2 * idx + th) % RING) * 512 + 512]
            else:
                dst = q_attn[:, idx * TT + th * 512: idx * TT + th * 512 + 512]

            def f_rope(r=r, th=th, dst=dst):
                def fn(v):
                    tsl = slice(th * 512, th * 512 + 512)
                    return [
                        v.tensor_tensor(tmpA[:, 0:512], praw[r % 2][:], trig1[:, tsl], ALU.mult),
                        v.tensor_tensor(tmpA[:, 512:1024], psr[r % 2][:], trig2[:, tsl], ALU.mult),
                        v.tensor_tensor(dst, tmpA[:, 0:512], tmpA[:, 512:1024], ALU.add),
                    ]
                return fn
            rope_ops[r] = P.add("vector", f_rope(), deps=[rot])
            if kind == "wk":
                piece_ops[2 * idx + th] = rope_ops[r]
            elif th == 1:
                q_ready[idx] = rope_ops[r]

    for i in range(3, len(slabs)):
        ld_w[i].deps.append(mm_ops[2 * (i - 3) + 1])

    vt_list = []
    nvt = 0
    for w in range(16, 32):
        tt_i = (w - 16) // 2
        s_i = (w - 16) % 2
        first_evc = None
        for jj in range(4):
            c8 = 4 * s_i + jj
            deps = [ev_ops[16 + 2 * c8], ev_ops[16 + 2 * c8 + 1]]
            if nvt >= 2:
                deps.append(vt_list[nvt - 2][1])
            tsp = P.add(
                "tensor",
                (lambda c8, tt_i, bI: lambda te: te.transpose(
                    pst1[bI][:],
                    vfm_all[:, c8 * 1024 + tt_i * 128: c8 * 1024 + tt_i * 128 + 128],
                    ident[:]))(c8, tt_i, nvt % 2),
                deps=deps)

            if jj == 0:
                P.add("vector",
                      (lambda w=w: lambda v: v.wait_ge(
                          sems["lsem"], 16 * (2 * (w - RING) + 1)))(),
                      deps=[])
            evc = P.add(
                "vector",
                (lambda w=w, jj=jj, bI=nvt % 2: lambda v: v.tensor_copy(
                    ring[:, (w % RING) * 512 + jj * 128:(w % RING) * 512 + jj * 128 + 128],
                    pst1[bI][:]))(),
                deps=[tsp])
            vt_list.append((tsp, evc))
            if first_evc is None:
                first_evc = evc
            nvt += 1
        piece_ops[w] = evc

    if _DEBUG["dump"]:
        d_xfm = P.add("gpsimd", lambda g: g.dma_start(out=dbg["xfm"], in_=x_fm[:]),
                      deps=[ld_xfm[-1]], dma=True)
        d_scr = P.add("gpsimd", lambda g: g.dma_start(out=dbg["scr"], in_=scr[:]),
                      deps=[c_trig], dma=True)
        d_tmpA = P.add("gpsimd", lambda g: g.dma_start(out=dbg["tmpA"], in_=tmpA[:]),
                       deps=[c_trig], dma=True)
        d_trig1 = P.add("gpsimd", lambda g: g.dma_start(out=dbg["trig"][:, 0:TT], in_=trig1[:]),
                        deps=[c_trig], dma=True)
        d_trig = P.add("gpsimd", lambda g: g.dma_start(out=dbg["trig"][:, TT:2 * TT], in_=trig2[:]),
                       deps=[c_trig], dma=True)
        d_ring = P.add("gpsimd", lambda g: g.dma_start(out=dbg["ring"], in_=ring[:]),
                       deps=[piece_ops[15]] + [rope_ops[r] for r in range(16)], dma=True)
        vt_list[0][1].deps.append(d_ring)
    for w in range(NWAVE):
        P.observe(piece_ops[w])

    def slot_chunks(r):
        return r % 4, 7 - (r % 4)

    last_exch = None
    for w in range(NWAVE):
        def f_wave(w=w, piece=piece_ops[w]):
            def fn(g):
                nm, val = piece.sem
                g.wait_ge(sems[nm], val)
                if w >= 1:
                    g.wait_ge(sems["asem"], 16 * w)
                src = ring[:, (w % RING) * 512:(w % RING) * 512 + 512]
                r = g.partition_id()
                for k in g.Switch(r, 8):
                    g.remote_dma_broadcast(
                        recv[:, k * 512:(k + 1) * 512], src, sems["dsem"], sems["lsem"],
                        rdests=[(0, d) for d in range(8)]).then_inc(sems["prep"], 1)
                g.wait_ge(sems["prep"], 2 * w + 1)
                g.trigger_dma(count=1)
                g.wait_ge(sems["dsem"], 16 * (w + 1))
                b_reg = r // 4
                for bb in g.Switch(b_reg, 2):
                    for rr in range(4 * bb, 4 * bb + 4):
                        cA, cB = slot_chunks(rr)
                        sl = recv[:, rr * 512:(rr + 1) * 512]
                        if w < 16:
                            kv, th = w // 2, w % 2
                            c = cA if th == 0 else cB
                            g.dma_start(out=k_all[kv, :, 512 * c:512 * c + 512],
                                        in_=sl).then_inc(sems["outd"], 16)
                        else:
                            tt_i, s_i = (w - 16) // 2, (w - 16) % 2
                            c = cA if tt_i < 4 else cB
                            row0 = 512 * c + 128 * (tt_i % 4)
                            g.dma_start(out=v_all[row0:row0 + 128, 512 * s_i:512 * s_i + 512],
                                        in_=sl).then_inc(sems["outd"], 16)
                g.wait_ge(sems["outd"], 16 * 4 * (w + 1))
                ack = g.remote_dma_broadcast(ackr[:], ackt[:], sems["asem"], sems["lsem"],
                                             rdests=[(0, d) for d in range(8)])
                ack.then_inc(sems["prep"], 1)
                g.wait_ge(sems["prep"], 2 * w + 2)
                g.trigger_dma(count=1)
                return None
            return fn
        last_exch = P.add("gpsimd", f_wave(), deps=[])

    exch_done = P.add("gpsimd",
                      lambda g: [g.wait_ge(sems["asem"], 16 * NWAVE),
                                 g.wait_ge(sems["lsem"], 16 * 2 * NWAVE),
                                 g.engine_nop()][-1],
                      deps=[])

    ph1.close()
    ph2 = contextlib.ExitStack()
    kg = [ph2.enter_context(nc.sbuf_tensor(f"kg{i}", [128, 8 * 512], BF16)) for i in range(2)]
    vg = [ph2.enter_context(nc.sbuf_tensor(f"vg{i}", [128, 8 * 512], BF16)) for i in range(2)]
    pt = [ph2.enter_context(nc.sbuf_tensor(f"pt{i}", [128, 512], BF16)) for i in range(3)]
    a_un = ph2.enter_context(nc.sbuf_tensor("a_un", [128, 512], BF16))
    a_tk = ph2.enter_context(nc.sbuf_tensor("a_tk", [128, 128], BF16))
    invd = ph2.enter_context(nc.sbuf_tensor("invd", [1, 1024], F32))
    invq4 = [ph2.enter_context(nc.sbuf_tensor(f"invq4_{i}", [128, 4], F32)) for i in range(2)]
    psS = [ph2.enter_context(nc.psum_tensor(f"psS{i}", [128, 512], F32)) for i in range(2)]
    psA = [ph2.enter_context(nc.psum_tensor(f"psA{i}", [128, 512], F32)) for i in range(2)]
    psDn = ph2.enter_context(nc.psum_tensor("psDn", [1, 512], F32))
    psT = ph2.enter_context(nc.psum_tensor("psT", [128, 128], BF16))
    psF = ph2.enter_context(nc.psum_tensor("psF", [128, 128], BF16))
    psDq = ph2.enter_context(nc.psum_tensor("psDq", [128, 4], F32))


    MAXC = [8, 7, 6, 5]
    attn_last = None
    ld_grp = []
    first_ld = {}
    for gI in range(_ATTN_GROUPS[0]):
        deps0 = [exch_done, rope_ops[79]] if gI < 2 else []
        grp_ops = []
        for c in range(8):
            op_ = P.add(
                "sync",
                (lambda gI, c: lambda sy: sy.dma_start(
                    out=kg[gI % 2][:, c * 512:(c + 1) * 512],
                    in_=k_all[gI, :, 512 * c:512 * c + 512]))(gI, c),
                deps=(deps0 if c == 0 else []), dma=True)
            grp_ops.append(op_)
            if c == 0:
                first_ld[gI] = op_
        for c in range(8):
            for j in range(4):
                grp_ops.append(P.add(
                    "sync",
                    (lambda gI, c, j: lambda sy: sy.dma_start(
                        out=vg[gI % 2][:, c * 512 + j * 128: c * 512 + j * 128 + 128],
                        in_=v_all[512 * c + 128 * j: 512 * c + 128 * j + 128,
                                  128 * gI:128 * gI + 128]))(gI, c, j),
                    deps=[], dma=True))
        ld_grp.append(grp_ops)
    with P.switch(4, ["tensor", "scalar", "vector"], 4) as body:
        for p in range(4):
            body(p)
            nch = 8
            att_ops = {}
            aw_ops = {}
            nrm_ops = {}
            dq_ops = {}
            uo_ops = {}
            rt_ops = {}
            fw_ops = {}
            exp_all = {}
            nS = 0
            natt = 0
            for gI in range(_ATTN_GROUPS[0]):
                for hh in range(4):
                    h = 4 * gI + hh
                    for (qsel, cq) in ((0, p), (1, 7 - p)):
                        qoff = h * TT + qsel * 512
                        natt += 1
                        nsub = 4 * (cq + 1)
                        si = 0
                        av_first = list(ld_grp[gI])
                        if natt >= 3:
                            av_first.append(aw_ops[natt - 2])
                        if natt >= 2:
                            av_first.append(nrm_ops[natt - 1])
                        av_last = None
                        for c in range(cq + 1):
                            for j in range(4):
                                nS += 1
                                diag = (c == cq)
                                deps_s = []
                                if si == 0:
                                    deps_s = ld_grp[gI] + [q_ready[h]]
                                if nS >= 3:
                                    deps_s.append(exp_all[nS - 2])
                                s_op = P.add(
                                    "tensor",
                                    (lambda gI, c, j, qoff, bI: lambda te: te.matmul(
                                        psS[bI][:],
                                        kg[gI % 2][:, c * 512 + j * 128: c * 512 + j * 128 + 128],
                                        q_attn[:, qoff:qoff + 512],
                                        start=True, stop=True))(gI, c, j, qoff, (nS - 1) % 2),
                                    deps=deps_s)
                                pre = s_op
                                if diag:
                                    pre = P.add(
                                        "vector",
                                        (lambda j, bI: lambda v: v.tensor_tensor(
                                            psS[bI][:], psS[bI][:],
                                            mask[:, 512 - 128 * j: 1024 - 128 * j],
                                            ALU.add))(j, (nS - 1) % 2),
                                        deps=[s_op])
                                e_op = P.add(
                                    "scalar",
                                    (lambda bI, bp: lambda s: s.activation(
                                        pt[bp][:], psS[bI][:], AF.Exp, bias=b12[:],
                                        scale=SCALE))((nS - 1) % 2, (nS - 1) % 3),
                                    deps=[pre])
                                exp_all[nS] = e_op
                                av = P.add(
                                    "tensor",
                                    (lambda gI, c, j, ba, bp, st_, sp_: lambda te: [
                                        te.matmul(psA[ba][:],
                                                  vg[gI % 2][:, c * 512 + j * 128:
                                                             c * 512 + j * 128 + 128],
                                                  pt[bp][:], start=st_, stop=sp_),
                                        te.matmul(psDn[:], onescol[:], pt[bp][:],
                                                  start=st_, stop=sp_)])(
                                        gI, c, j, (natt - 1) % 2, (nS - 1) % 3,
                                        si == 0, si == nsub - 1),
                                    deps=[e_op] + (av_first if si == 0 else []))
                                av_last = av
                                si += 1
                        att_ops[natt] = av_last
                        nrm_deps = [av_last]
                        if natt >= 3:
                            nrm_deps.append(dq_ops[natt - 2])
                        nrm = P.add(
                            "vector",
                            (lambda qsel: lambda v: v.reciprocal(
                                invd[0:1, qsel * 512:qsel * 512 + 512], psDn[:]))(qsel),
                            deps=nrm_deps)
                        nrm_ops[natt] = nrm
                        dq_deps = [nrm]
                        if natt >= 2:
                            dq_deps.append(uo_ops[natt - 1])
                        dq = P.add(
                            "tensor",
                            (lambda qsel: lambda te: [te.transpose(
                                psDq[:, jq:jq + 1],
                                invd[0:1, qsel * 512 + jq * 128: qsel * 512 + jq * 128 + 128],
                                identf[0:1, 0:1]) for jq in range(4)])(qsel),
                            deps=dq_deps)
                        dq_ops[natt] = dq
                        uo_deps = [dq]
                        if natt >= 3:
                            uo_deps.append(rt_ops[(natt - 2, 3)])
                        uo = P.add(
                            "scalar",
                            (lambda bI: lambda s: s.copy(invq4[bI][:], psDq[:]))((natt - 1) % 2),
                            deps=uo_deps)
                        uo_ops[natt] = uo
                        aw_deps = [av_last]
                        if natt >= 2:
                            aw_deps.append(rt_ops[(natt - 1, 3)])
                        aw = P.add(
                            "scalar",
                            (lambda ba: lambda s: s.copy(a_un[:], psA[ba][:]))((natt - 1) % 2),
                            deps=aw_deps)
                        aw_ops[natt] = aw
                        for jq in range(4):
                            tdeps = [aw]
                            if natt > 1 or jq > 0:
                                tdeps.append(rt_ops[(natt, jq - 1)] if jq >= 1
                                             else rt_ops[(natt - 1, 3)])
                            t1 = P.add(
                                "tensor",
                                (lambda jq: lambda te: te.transpose(
                                    psT[:], a_un[:, jq * 128:jq * 128 + 128], ident[:]))(jq),
                                deps=tdeps)
                            rdeps = [t1, uo]
                            if natt > 1 or jq > 0:
                                rdeps.append(fw_ops[(natt, jq - 1)] if jq >= 1
                                             else fw_ops[(natt - 1, 3)])
                            rt = P.add(
                                "vector",
                                (lambda bI, jq: lambda v: v.tensor_scalar(
                                    a_tk[:], psT[:], invq4[bI][:, jq:jq + 1], None,
                                    ALU.mult))((natt - 1) % 2, jq),
                                deps=rdeps)
                            rt_ops[(natt, jq)] = rt
                            t2 = P.add("tensor",
                                       lambda te: te.transpose(psF[:], a_tk[:], ident[:]),
                                       deps=[rt])
                            fw = P.add(
                                "scalar",
                                (lambda qoff, jq: lambda s: s.copy(
                                    q_attn[:, qoff + jq * 128: qoff + jq * 128 + 128],
                                    psF[:]))(qoff, jq),
                                deps=[t2])
                            fw_ops[(natt, jq)] = fw
            for gI in range(2, _ATTN_GROUPS[0]):
                first_ld[gI].deps.append(att_ops[8 * (gI - 2) + 8])
            P.observe(fw_ops[(8 * _ATTN_GROUPS[0], 3)])
            attn_last = fw_ops[(8 * _ATTN_GROUPS[0], 3)]

    ph2.close()
    ph23 = contextlib.ExitStack()
    y_fm = ph23.enter_context(nc.sbuf_tensor("y_fm", [128, NDC * TT], BF16))
    wo_sb = [ph23.enter_context(nc.sbuf_tensor(f"wo_sb{i}", [128, D], BF16)) for i in range(2)]
    psY = [ph23.enter_context(nc.psum_tensor(f"psY{i}", [128, 512], F32)) for i in range(2)]

    ld_wo = {}
    wo_mm = {}
    yev = {}
    gidx = 0
    for oc in range(NDC):
        wdeps = [mm_ops[NG - 1], exch_done, rope_ops[79], attn_last]
        if _DEBUG["dump"]:
            wdeps += [d_xfm, d_trig]
        if oc >= 2:
            wdeps.append(wo_mm[2 * (oc - 1)])
        ld_wo[oc] = P.add(
            "gpsimd",
            (lambda oc: lambda g: g.dma_start(out=wo_sb[oc % 2][:], in_=wo_n[oc]))(oc),
            deps=wdeps, dma=True)
        for th in range(2):
            gidx += 1
            mdeps = [ld_wo[oc], attn_last]
            if gidx >= 3:
                mdeps.append(yev[gidx - 2])

            def f_wo(oc=oc, th=th, bI=(gidx - 1) % 2):
                def fn(te):
                    return [te.matmul(
                        psY[bI][:], wo_sb[oc % 2][:, fc * 128:(fc + 1) * 128],
                        q_attn[:, fc * TT + th * 512: fc * TT + th * 512 + 512],
                        start=(fc == 0), stop=(fc == NDC - 1)) for fc in range(NDC)]
                return fn
            wo_mm[gidx] = P.add("tensor", f_wo(), deps=mdeps)
            yev[gidx] = P.add(
                "scalar",
                (lambda oc, th, bI: lambda s: s.copy(
                    y_fm[:, oc * TT + th * 512: oc * TT + th * 512 + 512],
                    psY[bI][:]))(oc, th, (gidx - 1) % 2),
                deps=[wo_mm[gidx]])
    y_done = yev[gidx]

    ph3 = contextlib.ExitStack()
    x1acc = ph3.enter_context(nc.sbuf_tensor("x1acc", [128, D], F32))
    xcent = ph3.enter_context(nc.sbuf_tensor("xcent", [128, D], F32))
    xres = ph3.enter_context(nc.sbuf_tensor("xres", [128, D], F32))
    lncol = ph3.enter_context(nc.sbuf_tensor("lncol", [128, 6], F32))
    psX1 = [ph3.enter_context(nc.psum_tensor(f"psX1_{i}", [128, 128], BF16)) for i in range(2)]

    ln_fin = {}
    spill3 = {}
    x1tev = {}
    nt3 = 0
    for tt in range(8):
        xdeps = [attn_last] if tt == 0 else [ln_fin[tt - 1]]
        xr = P.add("sync",
                   (lambda tt: lambda sy: sy.dma_start(
                       out=xres[:], in_=x_tok[tt * 128:(tt + 1) * 128, :]))(tt),
                   deps=xdeps, dma=True)
        ev_prev = None
        for oc in range(NDC):
            nt3 += 1
            tdeps = [y_done]
            if nt3 >= 3:
                tdeps.append(x1tev[nt3 - 2])
            if tt >= 1 and oc == 0:
                tdeps.append(spill3[tt - 1])
            t3 = P.add("tensor",
                       (lambda oc, tt, bI: lambda te: te.transpose(
                           psX1[bI][:],
                           y_fm[:, oc * TT + tt * 128: oc * TT + tt * 128 + 128],
                           ident[:]))(oc, tt, (nt3 - 1) % 2),
                       deps=tdeps)
            edeps = [t3, xr]
            if tt >= 1 and oc == 0:
                edeps.append(spill3[tt - 1])
            x1tev[nt3] = P.add(
                "vector",
                (lambda oc, bI: lambda v: v.tensor_tensor(
                    x1acc[:, oc * 128:(oc + 1) * 128], psX1[bI][:],
                    xres[:, oc * 128:(oc + 1) * 128], ALU.add))(oc, (nt3 - 1) % 2),
                deps=edeps)
        ln1 = P.add(
            "vector",
            lambda v: v.reduce_sum(lncol[:, 0:1], x1acc[:], AX.X),
            deps=[x1tev[nt3]])
        ln1b = P.add(
            "scalar",
            lambda s: s.activation(lncol[:, 4:5], lncol[:, 0:1], AF.Copy,
                                   scale=-1.0 / D),
            deps=[ln1])
        ln2 = P.add(
            "vector",
            lambda v: v.tensor_scalar(xcent[:], x1acc[:], lncol[:, 4:5], None,
                                      ALU.add),
            deps=[ln1b])
        ln3 = P.add(
            "scalar",
            lambda s: s.activation(x1acc[:], xcent[:], AF.Square,
                                   accum_out=lncol[:, 1:2]),
            deps=[ln2])
        ln3b = P.add(
            "vector",
            lambda v: v.tensor_copy(scr2[:, 6:7], beps[:]),
            deps=[ln3])
        ln4 = P.add(
            "scalar",
            lambda s: s.activation(lncol[:, 2:3], lncol[:, 1:2], AF.Sqrt,
                                   bias=beps[:], scale=1.0 / D),
            deps=[ln3b])
        ln4b = P.add(
            "vector",
            lambda v: v.reciprocal(lncol[:, 3:4], lncol[:, 2:3]),
            deps=[ln4])
        ln4c = P.add(
            "scalar",
            lambda s: s.activation(scr2[:, 6:7], beps[:], AF.Copy),
            deps=[ln4b])
        ln5 = P.add(
            "vector",
            lambda v: v.tensor_scalar(x1acc[:], xcent[:], lncol[:, 3:4], None,
                                      ALU.mult),
            deps=[ln4c])
        ln_fin[tt] = ln5
        spill3[tt] = P.add(
            "scalar",
            (lambda tt: lambda s: s.dma_start(
                out=x1_dram[tt * 128:(tt + 1) * 128, :], in_=x1acc[:]))(tt),
            deps=[ln5], dma=True)

    ph3.close()
    ph23.close()
    ph12.close()
    ph4 = contextlib.ExitStack()
    x1_fm = ph4.enter_context(nc.sbuf_tensor("x1_fm", [128, NDC * TT], BF16))
    wtb = [ph4.enter_context(nc.sbuf_tensor(f"wtb{i}", [128, NDC * 512], BF16)) for i in range(2)]
    ball = ph4.enter_context(nc.sbuf_tensor("ball", [128, D], BF16))
    at_sb = ph4.enter_context(nc.sbuf_tensor("at_sb", [128, NDC * 128], BF16))
    rwt_sb = ph4.enter_context(nc.sbuf_tensor("rwt_sb", [128, NDC * 8], BF16))
    bias_sb = ph4.enter_context(nc.sbuf_tensor("bias_sb", [1, D], BF16))
    rb_sb = ph4.enter_context(nc.sbuf_tensor("rb_sb", [1, 8], F32))
    u_fm = ph4.enter_context(nc.sbuf_tensor("u_fm", [128, TT], BF16))
    x1rd = ph4.enter_context(nc.sbuf_tensor("x1rd", [128, D], F32))
    x1bf = ph4.enter_context(nc.sbuf_tensor("x1bf", [128, D], BF16))
    qacc = ph4.enter_context(nc.sbuf_tensor("qacc", [128, 2 * NDC], F32))
    qbf = ph4.enter_context(nc.sbuf_tensor("qbf", [128, NDC], BF16))
    lgp = ph4.enter_context(nc.sbuf_tensor("lgp", [128, 8], F32))
    rtg = ph4.enter_context(nc.sbuf_tensor("rtg", [1, 8], F32))
    rtgb = ph4.enter_context(nc.sbuf_tensor("rtgb", [1, 8], BF16))
    rtT = ph4.enter_context(nc.sbuf_tensor("rtT", [8, 1], BF16))
    rexp = ph4.enter_context(nc.sbuf_tensor("rexp", [128, 1], F32))
    x1sl = [ph4.enter_context(nc.sbuf_tensor(f"x1sl{i}", [128, 512], F32)) for i in range(2)]
    o2ev = [ph4.enter_context(nc.sbuf_tensor(f"o2ev{i}", [128, 512], F32)) for i in range(2)]
    psF2 = [ph4.enter_context(nc.psum_tensor(f"psF2_{i}", [128, 128], BF16)) for i in range(2)]
    psR = ph4.enter_context(nc.psum_tensor("psR", [128, 32], F32))
    psL = ph4.enter_context(nc.psum_tensor("psL", [1, 8], F32))
    psRx = ph4.enter_context(nc.psum_tensor("psRx", [128, 1], BF16))
    psTA = ph4.enter_context(nc.psum_tensor("psTA", [128, 512], F32))
    psFF = [ph4.enter_context(nc.psum_tensor(f"psFF{i}", [128, 512], F32)) for i in range(2)]

    ld4 = {}
    for name, dst, src in (("at", at_sb, a_t2), ("ball", ball, ball_in),
                           ("rwt", rwt_sb, rwt2), ("bias", bias_sb, bias_in),
                           ("rb", rb_sb, rb_in)):
        ld4[name] = P.add("gpsimd",
                          (lambda dst, src: lambda g: g.dma_start(out=dst[:], in_=src))(dst, src),
                          deps=[exch_done] + list(spill3.values()), dma=True)

    x1c = {}
    rmm = {}
    qacc_add = {}
    t4e = {}
    nT4 = 0
    zop = P.add("vector", lambda v: v.memset(qacc[:, 0:NDC], 0.0),
                deps=list(spill3.values()))
    for tt in range(8):
        rdeps = list(spill3.values())
        if tt >= 1:
            rdeps.append(x1c[tt - 1])
        rd = P.add("sync",
                   (lambda tt: lambda sy: sy.dma_start(
                       out=x1rd[:], in_=x1_dram[tt * 128:(tt + 1) * 128, :]))(tt),
                   deps=rdeps, dma=True)
        cdeps = [rd]
        if tt >= 1:
            cdeps.append(rmm[tt - 1])
        x1c[tt] = P.add("vector", lambda v: v.tensor_copy(x1bf[:], x1rd[:]), deps=cdeps)
        for dc in range(NDC):
            nT4 += 1
            tdeps = [x1c[tt]]
            if nT4 >= 3:
                tdeps.append(t4e[nT4 - 2])
            t4 = P.add("tensor",
                       (lambda dc, bI: lambda te: te.transpose(
                           psF2[bI][:], x1bf[:, dc * 128:(dc + 1) * 128], ident[:]))(
                               dc, (nT4 - 1) % 2),
                       deps=tdeps)
            t4e[nT4] = P.add(
                "scalar",
                (lambda dc, tt, bI: lambda s: s.copy(
                    x1_fm[:, dc * TT + tt * 128: dc * TT + tt * 128 + 128], psF2[bI][:]))(
                        dc, tt, (nT4 - 1) % 2),
                deps=[t4])
        rdeps2 = [x1c[tt], c_vec, t4e[nT4]]
        if tt >= 1:
            rdeps2.append(qacc_add[tt - 1])

        def f_rmm(tt=tt):
            def fn(te):
                return [te.matmul(psR[:, dc:dc + 1], x1bf[:, dc * 128:(dc + 1) * 128],
                                  qm8[:, tt:tt + 1], start=True, stop=True)
                        for dc in range(NDC)]
            return fn
        rmm[tt] = P.add("tensor", f_rmm(), deps=rdeps2)
        qacc_add[tt] = P.add(
            "vector",
            (lambda tt: lambda v: v.tensor_tensor(
                qacc[:, NDC * ((tt + 1) % 2):NDC * ((tt + 1) % 2) + NDC],
                qacc[:, NDC * (tt % 2):NDC * (tt % 2) + NDC], psR[:], ALU.add))(tt),
            deps=[rmm[tt], zop])

    qbf_b = P.add("scalar", lambda s: s.activation(
        scr2[:, 5:6], beps[:], AF.Copy), deps=[qacc_add[7]])
    qbf_op = P.add("vector", lambda v: v.tensor_copy(qbf[:], qacc[:, 0:NDC]), deps=[qbf_b])
    lg_mm = P.add(
        "tensor",
        lambda te: [te.matmul(psL[:], qbf[:, dc:dc + 1], rwt_sb[:, dc * 8:(dc + 1) * 8],
                              start=(dc == 0), stop=(dc == NDC - 1)) for dc in range(NDC)],
        deps=[qbf_op, ld4["rwt"]])
    lgz = P.add("vector", lambda v: v.memset(lgp[:], 0.0), deps=list(spill3.values()))
    lg_ev = P.add("scalar", lambda s: s.copy(lgp[0:1, :], psL[:]), deps=[lg_mm, lgz])
    P.observe(lg_ev)

    def f_rbc(g):
        nm, val = lg_ev.sem
        g.wait_ge(sems[nm], val)
        r = g.partition_id()
        for k in g.Switch(r, 8):
            g.remote_dma_broadcast(rsl[:, 8 * k:8 * k + 8], lgp[:],
                                   sems["rsem"], sems["lsem"],
                                   rdests=[(0, d) for d in range(8)]).then_inc(sems["prep"], 1)
        g.wait_ge(sems["prep"], 2 * NWAVE + 1)
        g.trigger_dma(count=1)
        g.wait_ge(sems["rsem"], 16)
        return g.engine_nop()
    rbc = P.add("gpsimd", f_rbc, deps=[])
    P.observe(rbc)

    with P.switch(8, ["vector"], 8) as body:
        rsum_done = None
        for k8 in range(8):
            body(k8)
            bb = k8 // 4
            rsA = P.add(
                "vector",
                (lambda bb: lambda v: [
                    v.tensor_tensor(rtg2[:], rsl[0:1, 32 * bb:32 * bb + 8],
                                    rsl[0:1, 32 * bb + 8:32 * bb + 16], ALU.add),
                    v.tensor_tensor(rtg[:], rsl[0:1, 32 * bb + 16:32 * bb + 24],
                                    rsl[0:1, 32 * bb + 24:32 * bb + 32], ALU.add)])(bb),
                deps=[rbc])
            rsB = P.add("scalar", lambda s: s.activation(
                scr2[:, 5:6], beps[:], AF.Copy), deps=[rsA])
            rs_op = P.add(
                "vector",
                lambda v: v.tensor_tensor(lgp[0:1, 0:8], rtg2[:], rtg[:], ALU.add),
                deps=[rsB])
            P.observe(rs_op)
            if rsum_done is None:
                rsum_done = rs_op

    sm1 = P.add(
        "scalar",
        lambda s: s.activation(rtg2[:], lgp[0:1, 0:8], AF.Copy,
                               scale=1.0 / float(QE - QS)),
        deps=[rsum_done])
    sm1b = P.add(
        "vector",
        lambda v: v.tensor_tensor(rtg[:], rtg2[:], rb_sb[:], ALU.add),
        deps=[sm1, ld4["rb"]])
    sm1c = P.add("scalar", lambda s: s.activation(
        scr2[:, 5:6], beps[:], AF.Copy), deps=[sm1b])
    sm1d = P.add(
        "vector",
        lambda v: v.reduce_max(scr[0:1, 0:1], rtg[:], AX.X, negate=True),
        deps=[sm1c])
    sm2 = P.add("scalar",
                lambda s: s.activation(rtg2[:], rtg[:], AF.Exp, bias=scr[0:1, 0:1], scale=1.0),
                deps=[sm1d])
    sm2b = P.add(
        "vector",
        lambda v: v.reduce_sum(scr[0:1, 1:2], rtg2[:], AX.X),
        deps=[sm2])
    sm2b2 = P.add("scalar", lambda s: s.activation(
        scr2[:, 5:6], beps[:], AF.Copy), deps=[sm2b])
    sm2c = P.add(
        "vector",
        lambda v: v.reciprocal(scr[0:1, 2:3], scr[0:1, 1:2]),
        deps=[sm2b2])
    sm2d = P.add("scalar", lambda s: s.activation(
        scr2[:, 5:6], beps[:], AF.Copy), deps=[sm2c])
    sm3 = P.add(
        "vector",
        lambda v: [v.tensor_scalar(rtgb[:], rtg2[:], scr[0:1, 2:3], None, ALU.mult),
                   v.tensor_scalar(rtg[:], rtg2[:], scr[0:1, 2:3], None, ALU.mult)],
        deps=[sm2d])
    rx1 = P.add("tensor",
                lambda te: te.transpose(psRx[0:8, 0:1], rtgb[:], ident[0:1, 0:1]),
                deps=[sm3])
    rx2 = P.add("scalar", lambda s: s.copy(rtT[:], psRx[0:8, 0:1]), deps=[rx1])
    rx3 = P.add("tensor",
                lambda te: te.matmul(psFF[0][:, 0:1], selm[:], rtT[:], start=True, stop=True),
                deps=[rx2])
    rx4 = P.add("scalar", lambda s: s.mul(rexp[:], psFF[0][:, 0:1], 2.0), deps=[rx3])

    u_op = {}
    for th in range(2):
        tdeps = [t4e[nT4], ld4["at"]]
        if th == 1:
            tdeps.append(u_op[0])

        def f_ta(th=th):
            def fn(te):
                return [te.matmul(psTA[:], at_sb[:, dc * 128:(dc + 1) * 128],
                                  x1_fm[:, dc * TT + th * 512: dc * TT + th * 512 + 512],
                                  start=(dc == 0), stop=(dc == NDC - 1)) for dc in range(NDC)]
            return fn
        ta = P.add("tensor", f_ta(), deps=tdeps)
        u_op[th] = P.add(
            "vector",
            (lambda th: lambda v: v.tensor_scalar(
                u_fm[:, th * 512:(th + 1) * 512], psTA[:], rexp[:], None, ALU.mult))(th),
            deps=[ta, rx4])

    ld_wt = {}
    ffn_mm = {}
    o2e = {}
    o2s = {}
    idx4 = 0
    for osl in range(8):
        wdeps = [exch_done] + list(spill3.values())
        if osl >= 2:
            wdeps.append(ffn_mm[(osl - 2, 7)])
        ld_wt[osl] = P.add(
            "gpsimd",
            (lambda osl: lambda g: g.dma_start(out=wtb[osl % 2][:], in_=wt_l[osl]))(osl),
            deps=wdeps, dma=True)
        for tt in range(8):
            idx4 += 1
            ldeps = list(spill3.values())
            if idx4 >= 3:
                ldeps.append(o2e[idx4 - 2])
            xl = P.add(
                "sync",
                (lambda tt, osl, bI: lambda sy: sy.dma_start(
                    out=x1sl[bI][:],
                    in_=x1_dram[tt * 128:(tt + 1) * 128, osl * 512:(osl + 1) * 512]))(
                        tt, osl, (idx4 - 1) % 2),
                deps=ldeps, dma=True)
            mdeps = [ld_wt[osl], u_op[1], t4e[nT4], ld4["ball"], ld4["bias"]]
            if idx4 >= 3:
                mdeps.append(o2e[idx4 - 2])

            def f_ffn(tt=tt, osl=osl, bI=(idx4 - 1) % 2):
                def fn(te):
                    r = [te.matmul(
                        psFF[bI][:], x1_fm[:, dc * TT + tt * 128: dc * TT + tt * 128 + 128],
                        wtb[osl % 2][:, dc * 512:(dc + 1) * 512],
                        start=(dc == 0), stop=False) for dc in range(NDC)]
                    r.append(te.matmul(psFF[bI][:], u_fm[:, tt * 128:(tt + 1) * 128],
                                       ball[:, osl * 512:(osl + 1) * 512],
                                       start=False, stop=False))
                    r.append(te.matmul(psFF[bI][:], onesrow[:],
                                       bias_sb[0:1, osl * 512:(osl + 1) * 512],
                                       start=False, stop=True))
                    return r
                return fn
            ffn_mm[(osl, tt)] = P.add("tensor", f_ffn(), deps=mdeps)
            o2e[idx4] = P.add(
                "vector",
                (lambda bI: lambda v: v.tensor_tensor(o2ev[bI][:], psFF[bI][:], x1sl[bI][:],
                                                      ALU.add))((idx4 - 1) % 2),
                deps=[ffn_mm[(osl, tt)], xl])
            o2s[(osl, tt)] = P.add(
                "scalar",
                (lambda tt, osl, bI: lambda s: s.dma_start(
                    out=o2_dram[tt * 128:(tt + 1) * 128, osl * 512:(osl + 1) * 512],
                    in_=o2ev[bI][:]))(tt, osl, (idx4 - 1) % 2),
                deps=[o2e[idx4]], dma=True)

    ph4.close()
    ph5 = contextlib.ExitStack()
    o2r = ph5.enter_context(nc.sbuf_tensor("o2r", [128, D], F32))
    oc5 = ph5.enter_context(nc.sbuf_tensor("oc5", [128, D], F32))
    osq = ph5.enter_context(nc.sbuf_tensor("osq", [128, D], F32))
    lnc2 = ph5.enter_context(nc.sbuf_tensor("lnc2", [128, 6], F32))

    out_ops = []
    prev_l3 = None
    prev_wr = None
    for tt in range(8):
        deps = list(o2s.values())
        if prev_l3 is not None:
            deps.append(prev_l3)
        rd5 = P.add("sync",
                    (lambda tt: lambda sy: sy.dma_start(
                        out=o2r[:], in_=o2_dram[tt * 128:(tt + 1) * 128, :]))(tt),
                    deps=deps, dma=True)
        l1d = [rd5]
        if prev_wr is not None:
            l1d.append(prev_wr)
        l1 = P.add(
            "vector",
            lambda v: v.reduce_sum(lnc2[:, 0:1], o2r[:], AX.X),
            deps=l1d)
        l1b = P.add(
            "scalar",
            lambda s: s.activation(lnc2[:, 4:5], lnc2[:, 0:1], AF.Copy,
                                   scale=-1.0 / D),
            deps=[l1])
        l2 = P.add(
            "vector",
            lambda v: v.tensor_scalar(oc5[:], o2r[:], lnc2[:, 4:5], None, ALU.add),
            deps=[l1b])
        l2b = P.add(
            "scalar",
            lambda s: s.activation(o2r[:], oc5[:], AF.Square,
                                   accum_out=lnc2[:, 1:2]),
            deps=[l2])
        l2c = P.add(
            "vector",
            lambda v: v.tensor_copy(scr2[:, 7:8], beps[:]),
            deps=[l2b])
        l2d = P.add(
            "scalar",
            lambda s: s.activation(lnc2[:, 2:3], lnc2[:, 1:2], AF.Sqrt,
                                   bias=beps[:], scale=1.0 / D),
            deps=[l2c])
        l2e = P.add(
            "vector",
            lambda v: v.reciprocal(lnc2[:, 3:4], lnc2[:, 2:3]),
            deps=[l2d])
        l2f = P.add(
            "scalar",
            lambda s: s.activation(scr2[:, 7:8], beps[:], AF.Copy),
            deps=[l2e])
        l3 = P.add(
            "vector",
            lambda v: v.tensor_scalar(osq[:], oc5[:], lnc2[:, 3:4], None, ALU.mult),
            deps=[l2f])
        wr = P.add("scalar",
                   (lambda tt: lambda s: s.dma_start(
                       out=out_h[tt * 128:(tt + 1) * 128, :], in_=osq[:]))(tt),
                   deps=[l3], dma=True)
        out_ops.append(wr)
        prev_l3 = l3
        prev_wr = wr

    if _DEBUG["dump"]:
        P.add("gpsimd", lambda g: g.dma_start(out=dbg["kall"], in_=k_all[:, :, :]),
              deps=[exch_done], dma=True)
        P.add("gpsimd", lambda g: g.dma_start(out=dbg["vall"], in_=v_all[:, :]),
              deps=[exch_done], dma=True)
        P.add("gpsimd", lambda g: g.dma_start(out=dbg["attn"], in_=q_attn[:]),
              deps=[attn_last], dma=True)
        P.add("gpsimd", lambda g: g.dma_start(out=dbg["x1"], in_=x1_dram[:, :]),
              deps=[spill3[7]], dma=True)
        P.add("gpsimd", lambda g: g.dma_start(out=dbg["rout"], in_=rtg[:]),
              deps=[sm3, out_ops[-1]], dma=True)

    P.add("gpsimd", lambda g: g.engine_nop(), deps=list(out_ops))

    if trunc is not None:
        kept = set()
        if trunc >= 1000:
            for seg in P.segments:
                d = seg["ops"] if seg["kind"] == "plain" else None
                if d is None:
                    for b in seg["bodies"]:
                        for e in ENGS:
                            b[e] = [o for o in b[e] if o.order < trunc - 1000]
                else:
                    for e in ENGS:
                        d[e] = [o for o in d[e] if o.order < trunc - 1000]
        else:
            P.segments = P.segments[:trunc]
        for seg in P.segments:
            dicts = [seg["ops"]] if seg["kind"] == "plain" else seg["bodies"]
            for dd in dicts:
                for e in ENGS:
                    for o in dd[e]:
                        kept.add(id(o))
        for seg in P.segments:
            dicts = [seg["ops"]] if seg["kind"] == "plain" else seg["bodies"]
            for dd in dicts:
                for e in ENGS:
                    for o in dd[e]:
                        o.deps = [dp for dp in o.deps if id(dp) in kept]
    P.schedule()
    P.emit(nc, blk, sems)
    ph5.close()
    st.close()
    return nc, dbg


def _shard_inputs(inputs):
    X = np.asarray(inputs["L0_attn_X"], np.float32)
    Wq = np.asarray(inputs["L0_attn_Wq"], np.float32)
    Wk = np.asarray(inputs["L0_attn_Wk"], np.float32)
    Wv = np.asarray(inputs["L0_attn_Wv"], np.float32)
    Wo = np.asarray(inputs["L0_attn_Wo"], np.float32)
    W = np.asarray(inputs["L0_ffn_W"], np.float32)
    b = np.asarray(inputs["L0_ffn_b"], np.float32)
    A = np.asarray(inputs["L0_ffn_A"], np.float32)
    B = np.asarray(inputs["L0_ffn_B"], np.float32)
    rW = np.asarray(inputs["L0_ffn_router_W"], np.float32)
    rb = np.asarray(inputs["L0_ffn_router_b"], np.float32)

    wq_l = np.ascontiguousarray(Wq.reshape(NDC, 128, NH, 128).transpose(2, 1, 0, 3)
                                ).reshape(NH, 128, D)
    wk_l = np.ascontiguousarray(Wk.reshape(NDC, 128, NKV, 128).transpose(2, 1, 0, 3)
                                ).reshape(NKV, 128, D)
    wv_l = np.ascontiguousarray(Wv.reshape(NDC, 128, NKV, 128).transpose(2, 1, 0, 3)
                                ).reshape(NKV, 128, D)
    wo_n = np.ascontiguousarray(Wo.reshape(NDC, 128, NDC, 128).transpose(2, 1, 0, 3)
                                ).reshape(NDC, 128, D)
    wt_l = np.ascontiguousarray(W.reshape(8, 512, NDC, 128).transpose(0, 3, 2, 1)
                                ).reshape(8, 128, NDC * 512)
    A_all = A.reshape(128, D)
    a_t2 = np.ascontiguousarray(A_all.T.reshape(NDC, 128, 128).transpose(1, 0, 2)
                                ).reshape(128, NDC * 128)
    ball = np.ascontiguousarray(B.transpose(0, 2, 1).reshape(128, D))
    rwt2 = np.ascontiguousarray(rW.T.reshape(NDC, 128, 8).transpose(1, 0, 2)
                                ).reshape(128, NDC * 8)
    bias_row = np.ascontiguousarray(b.reshape(1, D))
    rb_row = np.ascontiguousarray(rb.reshape(1, 8))

    in_maps = []
    for c in range(8):
        bb = c // 4
        p = c % 4
        rows = np.r_[512 * p:512 * (p + 1), 512 * (7 - p):512 * (8 - p)]
        xc = np.ascontiguousarray(X[bb][rows])
        xfm = np.ascontiguousarray(xc.T)
        pos = rows.astype(np.float32)
        pos_c = np.ascontiguousarray(pos.reshape(8, 128).T)
        posb = np.ascontiguousarray(np.broadcast_to(pos.reshape(1, TT), (128, TT)))
        in_maps.append({
            "x_tok": xc, "x_fmT": xfm,
            "wq_l": wq_l, "wk_l": wk_l, "wv_l": wv_l, "wo_n": wo_n,
            "wt_l": wt_l, "a_t2": a_t2, "ball_in": ball, "bias_in": bias_row,
            "rwt2": rwt2, "rb_in": rb_row, "pos_in": pos_c, "posb_in": posb,
        })
    return in_maps


def kernel(**inputs):
    in_maps = _shard_inputs(inputs)
    nc, dbg = build_program()
    nc.finalize()
    res = run_bass_kernel_spmd(nc, in_maps, core_ids=list(range(8)))
    out = np.zeros((2, T, D), np.float32)
    for c in range(8):
        bb = c // 4
        p = c % 4
        o = res.results[c]["out_h"]
        out[bb, 512 * p:512 * (p + 1)] = o[0:512]
        out[bb, 512 * (7 - p):512 * (8 - p)] = o[512:1024]
    if _DEBUG["dump"]:
        kernel._last_debug = res.results
    return out


# version 11
# speedup vs baseline: 1.3316x; 1.0547x over previous
import contextlib
import math
import numpy as np
import concourse.bacc as bacc
import concourse.bass as bass
import concourse.mybir as mybir
from concourse.bass_utils import run_bass_kernel_spmd

F32 = mybir.dt.float32
BF16 = mybir.dt.bfloat16
AF = mybir.ActivationFunctionType
ALU = mybir.AluOpType
AX = mybir.AxisListType

D = 4096
T = 4096
NH = 32
NKV = 8
TT = 1024
NDC = 32
SHIFT = 12.0
SCALE = 1.0 / math.sqrt(128.0)
EPS = 1e-5
QS, QE = 611, T - 1
PI = math.pi
TWO_PI = 2.0 * math.pi
NWAVE = 32
RING = 16
TWO_PI_C1 = 6.28125
TWO_PI_C2 = float(np.float32(2.0 * math.pi - 6.28125))
TWO_PI_C3 = float(2.0 * math.pi - 6.28125 - np.float32(2.0 * math.pi - 6.28125))


_DEBUG = {"dump": False}
_ATTN_GROUPS = [8]

ENGS = ("tensor", "scalar", "vector", "sync", "gpsimd")


_OPN = [0]


class Op:
    __slots__ = ("eng", "fn", "deps", "dma", "sem", "order")

    def __init__(self, eng, fn, deps, dma):
        self.eng = eng
        self.fn = fn
        assert all(d is not None for d in deps), "None dep (forward reference?)"
        self.deps = list(deps)
        self.dma = dma
        self.sem = None
        self.order = _OPN[0]
        _OPN[0] += 1


class Prog:
    def __init__(self):
        self.segments = []
        self.forced = set()
        self._cur = None
        self._open_plain()

    def _open_plain(self):
        self._cur = {e: [] for e in ENGS}
        self.segments.append({"kind": "plain", "ops": self._cur})

    def add(self, eng, fn, deps=(), dma=False):
        op = Op(eng, fn, deps, dma)
        self._cur[eng].append(op)
        return op

    def observe(self, op):
        self.forced.add(id(op))

    @contextlib.contextmanager
    def switch(self, n, engines, mod):
        seg = {"kind": "switch", "n": n, "engines": list(engines), "mod": mod,
               "bodies": [{e: [] for e in ENGS} for _ in range(n)]}
        self.segments.append(seg)
        saved = self._cur

        def body(k):
            self._cur = seg["bodies"][k]
        try:
            yield body
        finally:
            self._cur = saved
            self._open_plain()

    DMA_POOL = {"tensor": 2, "scalar": 6, "vector": 4, "sync": 8, "gpsimd": 8}

    def schedule(self):
        observed = set(self.forced)

        def mark(ops_dict):
            for e in ENGS:
                for op in ops_dict[e]:
                    if op.dma:
                        observed.add(id(op))
                    for d in op.deps:
                        if d.eng != op.eng or d.dma:
                            observed.add(id(d))
        for seg in self.segments:
            if seg["kind"] == "plain":
                mark(seg["ops"])
            else:
                for b in seg["bodies"]:
                    mark(b)

        cnt = {}
        for e in ENGS:
            cnt[e + "_c"] = 0
            for i in range(self.DMA_POOL[e]):
                cnt[f"{e}_d{i}"] = 0
            cnt[e + "_dn"] = 0

        def walk(ops_dict, counters):
            for e in ENGS:
                for op in ops_dict[e]:
                    if id(op) in observed:
                        if op.dma:
                            i = counters[e + "_dn"] % self.DMA_POOL[e]
                            counters[e + "_dn"] += 1
                            nm = f"{e}_d{i}"
                            counters[nm] += 16
                            op.sem = (nm, counters[nm])
                        else:
                            nm = e + "_c"
                            counters[nm] += 1
                            op.sem = (nm, counters[nm])

        for seg in self.segments:
            if seg["kind"] == "plain":
                walk(seg["ops"], cnt)
            else:
                finals = []
                for b in seg["bodies"]:
                    c2 = dict(cnt)
                    walk(b, c2)
                    finals.append(c2)
                for c2 in finals[1:]:
                    assert c2 == finals[0], (
                        f"non-uniform switch tick counts: {finals[0]} vs {c2}")
                cnt = finals[0]
        self.final = dict(cnt)

    def emit(self, nc, blk, sems):
        ENGOBJ = {"tensor": nc.tensor, "scalar": nc.scalar, "vector": nc.vector,
                  "sync": nc.sync, "gpsimd": nc.gpsimd}

        def emit_op(op):
            E = ENGOBJ[op.eng]
            waits = {}
            for d in op.deps:
                if d.eng == op.eng and not d.dma:
                    continue
                assert d.sem is not None, "dep never scheduled"
                nm, v = d.sem
                waits[nm] = max(waits.get(nm, 0), v)
            if op.sem is not None and op.dma:
                nm, v = op.sem
                if v > 16:
                    waits[nm] = max(waits.get(nm, 0), v - 16)
            for nm, v in waits.items():
                E.wait_ge(sems[nm], v)
            r = op.fn(E)
            if r is None:
                instrs = []
            elif isinstance(r, (list, tuple)):
                instrs = [i for i in r if i is not None]
            else:
                instrs = [r]
            if instrs:
                if op.sem is not None:
                    nm, _ = op.sem
                    instrs[-1].then_inc(sems[nm], 16 if op.dma else 1)
            else:
                assert op.sem is None

        def em(_):
            for seg in self.segments:
                if seg["kind"] == "plain":
                    for e in ENGS:
                        for op in seg["ops"][e]:
                            emit_op(op)
                else:
                    engs = [e for e in ENGS
                            if any(seg["bodies"][k][e] for k in range(seg["n"]))]
                    if not engs:
                        continue
                    eobjs = [ENGOBJ[e] for e in engs]
                    idx = [E.partition_id() % seg["mod"] for E in eobjs]
                    for k in nc.Switch(engines=eobjs, index=idx, n=seg["n"]):
                        for e in engs:
                            for op in seg["bodies"][k][e]:
                                emit_op(op)
        blk.gpsimd(em)


def build_program(trunc=None):
    nc = bacc.Bacc("TRN2", target_bir_lowering=False, num_devices=8,
                   detect_race_conditions=False)
    st = contextlib.ExitStack()

    x_tok = nc.dram_tensor("x_tok", [TT, D], F32, kind="ExternalInput").ap()
    x_fmT = nc.dram_tensor("x_fmT", [D, TT], F32, kind="ExternalInput").ap()
    wq_l = nc.dram_tensor("wq_l", [NH, 128, D], F32, kind="ExternalInput").ap()
    wk_l = nc.dram_tensor("wk_l", [NKV, 128, D], F32, kind="ExternalInput").ap()
    wv_l = nc.dram_tensor("wv_l", [NKV, 128, D], F32, kind="ExternalInput").ap()
    wo_n = nc.dram_tensor("wo_n", [NDC, 128, D], F32, kind="ExternalInput").ap()
    wt_l = nc.dram_tensor("wt_l", [8, 128, NDC * 512], F32, kind="ExternalInput").ap()
    a_t2 = nc.dram_tensor("a_t2", [128, NDC * 128], F32, kind="ExternalInput").ap()
    ball_in = nc.dram_tensor("ball_in", [128, D], F32, kind="ExternalInput").ap()
    bias_in = nc.dram_tensor("bias_in", [1, D], F32, kind="ExternalInput").ap()
    rwt2 = nc.dram_tensor("rwt2", [128, NDC * 8], F32, kind="ExternalInput").ap()
    rb_in = nc.dram_tensor("rb_in", [1, 8], F32, kind="ExternalInput").ap()
    pos_in = nc.dram_tensor("pos_in", [128, 8], F32, kind="ExternalInput").ap()
    posb_in = nc.dram_tensor("posb_in", [128, TT], F32, kind="ExternalInput").ap()
    out_h = nc.dram_tensor("out_h", [TT, D], F32, kind="ExternalOutput").ap()

    k_all = nc.dram_tensor("k_all", [NKV, 128, T], BF16, kind="Internal").ap()
    v_all = nc.dram_tensor("v_all", [T, 1024], BF16, kind="Internal").ap()
    x1_dram = nc.dram_tensor("x1_dram", [TT, D], F32, kind="Internal").ap()
    o2_dram = nc.dram_tensor("o2_dram", [TT, D], F32, kind="Internal").ap()

    dbg = {}
    if _DEBUG["dump"]:
        dbg["q"] = nc.dram_tensor("dbg_q", [128, NH * TT], F32, kind="ExternalOutput").ap()
        dbg["kall"] = nc.dram_tensor("dbg_kall", [NKV, 128, T], F32, kind="ExternalOutput").ap()
        dbg["vall"] = nc.dram_tensor("dbg_vall", [T, 1024], F32, kind="ExternalOutput").ap()
        dbg["attn"] = nc.dram_tensor("dbg_attn", [128, NH * TT], F32, kind="ExternalOutput").ap()
        dbg["x1"] = nc.dram_tensor("dbg_x1", [TT, D], F32, kind="ExternalOutput").ap()
        dbg["rout"] = nc.dram_tensor("dbg_rout", [1, 8], F32, kind="ExternalOutput").ap()
        dbg["xfm"] = nc.dram_tensor("dbg_xfm", [128, NDC * TT], F32, kind="ExternalOutput").ap()
        dbg["trig"] = nc.dram_tensor("dbg_trig", [128, 2 * TT], F32, kind="ExternalOutput").ap()
        dbg["ring"] = nc.dram_tensor("dbg_ring", [128, RING * 512], F32, kind="ExternalOutput").ap()
        dbg["scr"] = nc.dram_tensor("dbg_scr", [128, 8], F32, kind="ExternalOutput").ap()
        dbg["tmpA"] = nc.dram_tensor("dbg_tmpA", [128, TT], F32, kind="ExternalOutput").ap()

    sems = {}
    for e in ENGS:
        sems[e + "_c"] = st.enter_context(nc.semaphore(e + "_c"))
        for i in range(Prog.DMA_POOL[e]):
            sems[f"{e}_d{i}"] = st.enter_context(nc.semaphore(f"{e}_d{i}"))
    for n in ("prep", "dsem", "asem", "lsem", "rsem", "outd"):
        sems[n] = st.enter_context(nc.semaphore(n))

    blk = st.enter_context(nc.Block())
    P = Prog()

    ident = st.enter_context(nc.sbuf_tensor("ident", [128, 128], BF16))
    identf = st.enter_context(nc.sbuf_tensor("identf", [128, 128], F32))
    rperm = st.enter_context(nc.sbuf_tensor("rperm", [128, 128], BF16))
    selm = st.enter_context(nc.sbuf_tensor("selm", [8, 128], BF16))
    onescol = st.enter_context(nc.sbuf_tensor("onescol", [128, 1], BF16))
    onesrow = st.enter_context(nc.sbuf_tensor("onesrow", [1, 128], BF16))
    b12 = st.enter_context(nc.sbuf_tensor("b12", [128, 1], F32))
    beps = st.enter_context(nc.sbuf_tensor("beps", [128, 1], F32))
    pos8 = st.enter_context(nc.sbuf_tensor("pos8", [128, 8], F32))
    qm8 = st.enter_context(nc.sbuf_tensor("qm8", [128, 8], BF16))
    scr = st.enter_context(nc.sbuf_tensor("scr", [128, 8], F32))
    scr2 = st.enter_context(nc.sbuf_tensor("scr2", [128, 8], F32))
    rtg2 = st.enter_context(nc.sbuf_tensor("rtg2", [1, 8], F32))
    recv = st.enter_context(nc.sbuf_tensor("recv", [128, 8 * 512], BF16))
    ackt = st.enter_context(nc.sbuf_tensor("ackt", [128, 8], F32))
    ackr = st.enter_context(nc.sbuf_tensor("ackr", [128, 8], F32))
    rsl = st.enter_context(nc.sbuf_tensor("rsl", [128, 64], F32))

    ph12 = contextlib.ExitStack()
    mask = ph12.enter_context(nc.sbuf_tensor("mask", [128, 1024], F32))
    q_attn = ph12.enter_context(nc.sbuf_tensor("q_attn", [128, NH * TT], BF16))

    ph1 = contextlib.ExitStack()
    trig1 = ph1.enter_context(nc.sbuf_tensor("trig1", [128, TT], F32))
    trig2 = ph1.enter_context(nc.sbuf_tensor("trig2", [128, TT], F32))
    tmpA = ph1.enter_context(nc.sbuf_tensor("tmpA", [128, TT], F32))
    tmpB = ph1.enter_context(nc.sbuf_tensor("tmpB", [128, 512], F32))
    ph0 = contextlib.ExitStack()
    kA = ph0.enter_context(nc.sbuf_tensor("kA", [128, TT], F32))
    kB = ph0.enter_context(nc.sbuf_tensor("kB", [128, TT], F32))
    ph0.close()
    x_fm = ph1.enter_context(nc.sbuf_tensor("x_fm", [128, NDC * TT], BF16))
    w_sb = [ph1.enter_context(nc.sbuf_tensor(f"w_sb{i}", [128, D], BF16)) for i in range(2)]
    ring = ph1.enter_context(nc.sbuf_tensor("ring", [128, RING * 512], BF16))
    praw = [ph1.enter_context(nc.sbuf_tensor(f"praw{i}", [128, 512], BF16)) for i in range(2)]
    vfm_all = ph1.enter_context(nc.sbuf_tensor("vfm_all", [128, 8 * 1024], BF16))
    ps1 = [ph1.enter_context(nc.psum_tensor(f"ps1_{i}", [128, 512], F32)) for i in range(2)]
    psr = [ph1.enter_context(nc.psum_tensor(f"psr_{i}", [128, 512], F32)) for i in range(2)]
    pst1 = [ph1.enter_context(nc.psum_tensor(f"pst1_{i}", [128, 128], BF16)) for i in range(2)]

    c_pos8 = P.add("gpsimd", lambda g: g.dma_start(out=pos8[:], in_=pos_in), dma=True)
    c_posb = P.add("gpsimd", lambda g: g.dma_start(out=tmpA[:], in_=posb_in), dma=True)
    c_iota = P.add("gpsimd", lambda g: [
        g.iota(ident[:], pattern=[[1, 128]], base=0, channel_multiplier=-1,
               allow_small_or_imprecise_dtypes=True),
        g.iota(identf[:], pattern=[[1, 128]], base=0, channel_multiplier=-1,
               allow_small_or_imprecise_dtypes=True),
        g.iota(rperm[:], pattern=[[-1, 128]], base=-64, channel_multiplier=1,
               allow_small_or_imprecise_dtypes=True),
        g.iota(selm[:], pattern=[[1, 128]], base=0, channel_multiplier=-16,
               allow_small_or_imprecise_dtypes=True),
        g.iota(mask[:], pattern=[[1, 1024]], base=-512, channel_multiplier=-1,
               allow_small_or_imprecise_dtypes=True),
        g.iota(trig1[:, 0:1], pattern=[[0, 1]], base=0, channel_multiplier=1,
               allow_small_or_imprecise_dtypes=True),
        g.iota(trig1[:, 1:2], pattern=[[0, 1]], base=0, channel_multiplier=1,
               allow_small_or_imprecise_dtypes=True),
    ])

    def f_consts(v):
        return [
            v.tensor_single_scalar(ident[:], ident[:], 0.0, ALU.is_equal),
            v.tensor_single_scalar(identf[:], identf[:], 0.0, ALU.is_equal),
            v.tensor_single_scalar(tmpB[:, 0:128], rperm[:], -128.0, ALU.is_equal),
            v.tensor_single_scalar(tmpB[:, 128:256], rperm[:], 0.0, ALU.is_equal),
            v.tensor_tensor(rperm[:], tmpB[:, 0:128], tmpB[:, 128:256], ALU.add),
            v.tensor_single_scalar(tmpB[0:8, 0:128], selm[:], 0.0, ALU.is_ge),
            v.tensor_single_scalar(tmpB[0:8, 128:256], selm[:], 15.0, ALU.is_le),
            v.tensor_tensor(selm[:], tmpB[0:8, 0:128], tmpB[0:8, 128:256], ALU.mult),
            v.tensor_single_scalar(trig2[:], mask[:], 0.0, ALU.is_ge),
            v.tensor_scalar(mask[:], trig2[:, 0:1024], 1.0, 1.0e30, ALU.subtract, ALU.mult),
            v.memset(onescol[:], 1.0),
            v.memset(ackt[:], 0.0),
            v.memset(rsl[:], 0.0),
            v.memset(onesrow[:], 1.0),
            v.memset(b12[:], -SHIFT),
            v.memset(beps[:], EPS),
            v.tensor_single_scalar(tmpB[:, 0:8], pos8[:], float(QS), ALU.is_ge),
            v.tensor_single_scalar(scr2[:], pos8[:], float(QE - 1), ALU.is_le),
            v.tensor_scalar(scr[:, 4:5], trig1[:, 0:1], 63.5,
                            math.log(10000.0), ALU.is_ge, ALU.mult),
            v.tensor_scalar(scr[:, 6:7], trig1[:, 1:2], 63.5, 2.0,
                            ALU.is_ge, ALU.mult),
        ]
    c_vec = P.add("vector", f_consts, deps=[c_pos8, c_iota])
    c_if = P.add("scalar", lambda s: s.activation(
        scr[:, 3:4], trig1[:, 0:1], AF.Exp, bias=scr[:, 4:5],
        scale=-math.log(10000.0) / 64.0),
        deps=[c_vec])
    c_qm8 = P.add("vector", lambda v: v.tensor_tensor(
        qm8[:], tmpB[:, 0:8], scr2[:], ALU.mult), deps=[c_if])

    RED_BITS = [512, 256, 128, 64, 32, 16, 8, 4, 2, 1]

    def red_steps(v, cur, oth, w):
        r = []
        for i, b in enumerate(RED_BITS):
            thr = float(np.float32(b * TWO_PI))
            a, o = (cur, oth) if i % 2 == 0 else (oth, cur)
            r.append(v.tensor_scalar(w[:], a[:], thr, thr, ALU.is_ge, ALU.mult))
            r.append(v.tensor_tensor(o[:], a[:], w[:], ALU.subtract))
        r.append(v.tensor_scalar(w[:], cur[:], float(PI),
                                 float(np.float32(TWO_PI)), ALU.is_ge, ALU.mult))
        r.append(v.tensor_tensor(oth[:], cur[:], w[:], ALU.subtract))
        return r

    def f_ang(v):
        return [
            v.tensor_scalar(kB[:], tmpA[:], scr[:, 3:4], None, ALU.mult),
            v.tensor_single_scalar(trig1[:], kB[:], PI / 2.0, ALU.add),
        ] + red_steps(v, trig1, tmpA, kA)
    c_ang = P.add("vector", f_ang, deps=[c_if, c_posb])
    c_sin1 = P.add("scalar", lambda s: s.activation(trig1[:], tmpA[:], AF.Sin),
                   deps=[c_ang])
    c_ang2 = P.add("vector", lambda v: [v.tensor_copy(trig2[:], kB[:])]
                   + red_steps(v, trig2, tmpA, kA), deps=[c_sin1])
    c_sin2 = P.add("scalar", lambda s: s.activation(kA[:], tmpA[:], AF.Sin),
                   deps=[c_ang2])
    c_trig = P.add("vector", lambda v: [
        v.tensor_scalar(tmpA[:], kA[:], scr[:, 6:7], None, ALU.mult),
        v.tensor_tensor(trig2[:], tmpA[:], kA[:], ALU.subtract)], deps=[c_sin2])

    plan = ([("wk", kv, th) for kv in range(NKV) for th in range(2)] +
            [("wv", c8, th) for c8 in range(NKV) for th in range(2)] +
            [("wq", h, th) for h in range(NH) for th in range(2)])
    NG = len(plan)
    rope_gis = [gi for gi, pl in enumerate(plan) if pl[0] in ("wk", "wq")]

    ld_xfm = []
    for dc in range(NDC):
        ld_xfm.append(P.add(
            "gpsimd",
            (lambda dc: lambda g: g.dma_start(
                out=x_fm[:, dc * TT:(dc + 1) * TT],
                in_=x_fmT[dc * 128:(dc + 1) * 128, :]))(dc),
            deps=[c_trig], dma=True))

    slabs = ([("wk", wk_l, kv) for kv in range(NKV)] +
             [("wv", wv_l, c8) for c8 in range(NKV)] +
             [("wq", wq_l, h) for h in range(NH)])
    mm_ops = [None] * NG
    ev_ops = [None] * NG
    rope_ops = {}
    piece_ops = [None] * NWAVE
    q_ready = {}
    ld_w = []
    for i, (kind, src, idx) in enumerate(slabs):
        deps = [c_trig]
        ld_w.append(P.add(
            "gpsimd",
            (lambda src, idx, bI: lambda g: g.dma_start(out=w_sb[bI][:], in_=src[idx]))(
                src, idx, i % 2),
            deps=deps, dma=True))

    for gi, (kind, idx, th) in enumerate(plan):
        slab = gi // 2
        deps = [ld_w[slab]] + ld_xfm + ([ev_ops[gi - 2]] if gi >= 2 else [])

        def f_mm(gi=gi, slab=slab, th=th):
            def fn(te):
                ps = ps1[gi % 2]
                return [te.matmul(ps[:], w_sb[slab % 2][:, dc * 128:(dc + 1) * 128],
                                  x_fm[:, dc * TT + th * 512: dc * TT + th * 512 + 512],
                                  start=(dc == 0), stop=(dc == NDC - 1))
                        for dc in range(NDC)]
            return fn
        mm_ops[gi] = P.add("tensor", f_mm(), deps=deps)

        if kind == "wv":
            ev_ops[gi] = P.add(
                "scalar",
                (lambda gi, idx, th: lambda s: s.copy(
                    vfm_all[:, idx * 1024 + th * 512: idx * 1024 + th * 512 + 512],
                    ps1[gi % 2][:]))(gi, idx, th),
                deps=[mm_ops[gi]])
        else:
            r = rope_gis.index(gi)
            deps_e = [mm_ops[gi]] + ([rope_ops[r - 2]] if r >= 2 else [])
            ev_ops[gi] = P.add(
                "scalar",
                (lambda gi, r: lambda s: s.copy(praw[r % 2][:], ps1[gi % 2][:]))(gi, r),
                deps=deps_e)
            deps_r = [ev_ops[gi]] + ([rope_ops[r - 2]] if r >= 2 else [])
            rot = P.add(
                "tensor",
                (lambda r: lambda te: te.matmul(psr[r % 2][:], rperm[:], praw[r % 2][:],
                                                start=True, stop=True))(r),
                deps=deps_r)
            if kind == "wk":
                dst = ring[:, (2 * idx + th) % RING * 512:
                           ((2 * idx + th) % RING) * 512 + 512]
            else:
                dst = q_attn[:, idx * TT + th * 512: idx * TT + th * 512 + 512]

            def f_rope(r=r, th=th, dst=dst):
                def fn(v):
                    tsl = slice(th * 512, th * 512 + 512)
                    return [
                        v.tensor_tensor(tmpA[:, 0:512], praw[r % 2][:], trig1[:, tsl], ALU.mult),
                        v.tensor_tensor(tmpB[:, 0:512], psr[r % 2][:], trig2[:, tsl], ALU.mult),
                        v.tensor_tensor(dst, tmpA[:, 0:512], tmpB[:, 0:512], ALU.add),
                    ]
                return fn
            rope_ops[r] = P.add("vector", f_rope(), deps=[rot])
            if kind == "wk":
                piece_ops[2 * idx + th] = rope_ops[r]
            elif th == 1:
                q_ready[idx] = rope_ops[r]

    for i in range(2, len(slabs)):
        ld_w[i].deps.append(mm_ops[2 * (i - 2) + 1])

    vt_list = []
    nvt = 0
    for w in range(16, 32):
        tt_i = (w - 16) // 2
        s_i = (w - 16) % 2
        first_evc = None
        for jj in range(4):
            c8 = 4 * s_i + jj
            deps = [ev_ops[16 + 2 * c8], ev_ops[16 + 2 * c8 + 1]]
            if nvt >= 2:
                deps.append(vt_list[nvt - 2][1])
            tsp = P.add(
                "tensor",
                (lambda c8, tt_i, bI: lambda te: te.transpose(
                    pst1[bI][:],
                    vfm_all[:, c8 * 1024 + tt_i * 128: c8 * 1024 + tt_i * 128 + 128],
                    ident[:]))(c8, tt_i, nvt % 2),
                deps=deps)

            if jj == 0:
                P.add("vector",
                      (lambda w=w: lambda v: v.wait_ge(
                          sems["lsem"], 16 * (2 * (w - RING) + 1)))(),
                      deps=[])
            evc = P.add(
                "vector",
                (lambda w=w, jj=jj, bI=nvt % 2: lambda v: v.tensor_copy(
                    ring[:, (w % RING) * 512 + jj * 128:(w % RING) * 512 + jj * 128 + 128],
                    pst1[bI][:]))(),
                deps=[tsp])
            vt_list.append((tsp, evc))
            if first_evc is None:
                first_evc = evc
            nvt += 1
        piece_ops[w] = evc

    if _DEBUG["dump"]:
        d_xfm = P.add("gpsimd", lambda g: g.dma_start(out=dbg["xfm"], in_=x_fm[:]),
                      deps=[ld_xfm[-1]], dma=True)
        d_scr = P.add("gpsimd", lambda g: g.dma_start(out=dbg["scr"], in_=scr[:]),
                      deps=[c_trig], dma=True)
        d_tmpA = P.add("gpsimd", lambda g: g.dma_start(out=dbg["tmpA"], in_=tmpA[:]),
                       deps=[c_trig], dma=True)
        d_trig1 = P.add("gpsimd", lambda g: g.dma_start(out=dbg["trig"][:, 0:TT], in_=trig1[:]),
                        deps=[c_trig], dma=True)
        d_trig = P.add("gpsimd", lambda g: g.dma_start(out=dbg["trig"][:, TT:2 * TT], in_=trig2[:]),
                       deps=[c_trig], dma=True)
        d_ring = P.add("gpsimd", lambda g: g.dma_start(out=dbg["ring"], in_=ring[:]),
                       deps=[piece_ops[15]] + [rope_ops[r] for r in range(16)], dma=True)
        vt_list[0][1].deps.append(d_ring)
    for w in range(NWAVE):
        P.observe(piece_ops[w])

    def slot_chunks(r):
        return r % 4, 7 - (r % 4)

    last_exch = None
    for w in range(NWAVE):
        def f_wave(w=w, piece=piece_ops[w]):
            def fn(g):
                nm, val = piece.sem
                g.wait_ge(sems[nm], val)
                if w >= 1:
                    g.wait_ge(sems["asem"], 16 * w)
                src = ring[:, (w % RING) * 512:(w % RING) * 512 + 512]
                r = g.partition_id()
                for k in g.Switch(r, 8):
                    g.remote_dma_broadcast(
                        recv[:, k * 512:(k + 1) * 512], src, sems["dsem"], sems["lsem"],
                        rdests=[(0, d) for d in range(8)]).then_inc(sems["prep"], 1)
                g.wait_ge(sems["prep"], 2 * w + 1)
                g.trigger_dma(count=1)
                g.wait_ge(sems["dsem"], 16 * (w + 1))
                b_reg = r // 4
                for bb in g.Switch(b_reg, 2):
                    for rr in range(4 * bb, 4 * bb + 4):
                        cA, cB = slot_chunks(rr)
                        sl = recv[:, rr * 512:(rr + 1) * 512]
                        if w < 16:
                            kv, th = w // 2, w % 2
                            c = cA if th == 0 else cB
                            g.dma_start(out=k_all[kv, :, 512 * c:512 * c + 512],
                                        in_=sl).then_inc(sems["outd"], 16)
                        else:
                            tt_i, s_i = (w - 16) // 2, (w - 16) % 2
                            c = cA if tt_i < 4 else cB
                            row0 = 512 * c + 128 * (tt_i % 4)
                            g.dma_start(out=v_all[row0:row0 + 128, 512 * s_i:512 * s_i + 512],
                                        in_=sl).then_inc(sems["outd"], 16)
                g.wait_ge(sems["outd"], 16 * 4 * (w + 1))
                ack = g.remote_dma_broadcast(ackr[:], ackt[:], sems["asem"], sems["lsem"],
                                             rdests=[(0, d) for d in range(8)])
                ack.then_inc(sems["prep"], 1)
                g.wait_ge(sems["prep"], 2 * w + 2)
                g.trigger_dma(count=1)
                return None
            return fn
        last_exch = P.add("gpsimd", f_wave(), deps=[])

    exch_done = P.add("gpsimd",
                      lambda g: [g.wait_ge(sems["asem"], 16 * NWAVE),
                                 g.wait_ge(sems["lsem"], 16 * 2 * NWAVE),
                                 g.engine_nop()][-1],
                      deps=[])

    ph1.close()
    ph2 = contextlib.ExitStack()
    kg = [ph2.enter_context(nc.sbuf_tensor(f"kg{i}", [128, 8 * 512], BF16)) for i in range(3)]
    vg = [ph2.enter_context(nc.sbuf_tensor(f"vg{i}", [128, 8 * 512], BF16)) for i in range(3)]
    pt = [ph2.enter_context(nc.sbuf_tensor(f"pt{i}", [128, 512], BF16)) for i in range(3)]
    a_un = ph2.enter_context(nc.sbuf_tensor("a_un", [128, 512], BF16))
    a_tk = ph2.enter_context(nc.sbuf_tensor("a_tk", [128, 128], BF16))
    invd = ph2.enter_context(nc.sbuf_tensor("invd", [1, 1024], F32))
    invq4 = [ph2.enter_context(nc.sbuf_tensor(f"invq4_{i}", [128, 4], F32)) for i in range(2)]
    psS = [ph2.enter_context(nc.psum_tensor(f"psS{i}", [128, 512], F32)) for i in range(2)]
    psA = [ph2.enter_context(nc.psum_tensor(f"psA{i}", [128, 512], F32)) for i in range(2)]
    psDn = ph2.enter_context(nc.psum_tensor("psDn", [1, 512], F32))
    psT = ph2.enter_context(nc.psum_tensor("psT", [128, 128], BF16))
    psF = ph2.enter_context(nc.psum_tensor("psF", [128, 128], BF16))
    psDq = ph2.enter_context(nc.psum_tensor("psDq", [128, 4], F32))


    MAXC = [8, 7, 6, 5]
    attn_last = None
    ld_grp = []
    first_ld = {}
    for gI in range(_ATTN_GROUPS[0]):
        deps0 = [exch_done, rope_ops[79]] if gI < 2 else []
        grp_ops = []
        for c in range(8):
            op_ = P.add(
                "sync",
                (lambda gI, c: lambda sy: sy.dma_start(
                    out=kg[gI % 3][:, c * 512:(c + 1) * 512],
                    in_=k_all[gI, :, 512 * c:512 * c + 512]))(gI, c),
                deps=(deps0 if c == 0 else []), dma=True)
            grp_ops.append(op_)
            if c == 0:
                first_ld[gI] = op_
        for c in range(8):
            for j in range(4):
                grp_ops.append(P.add(
                    "sync",
                    (lambda gI, c, j: lambda sy: sy.dma_start(
                        out=vg[gI % 3][:, c * 512 + j * 128: c * 512 + j * 128 + 128],
                        in_=v_all[512 * c + 128 * j: 512 * c + 128 * j + 128,
                                  128 * gI:128 * gI + 128]))(gI, c, j),
                    deps=[], dma=True))
        ld_grp.append(grp_ops)
    with P.switch(4, ["tensor", "scalar", "vector"], 4) as body:
        for p in range(4):
            body(p)
            nch = 8
            att_ops = {}
            aw_ops = {}
            nrm_ops = {}
            dq_ops = {}
            uo_ops = {}
            rt_ops = {}
            fw_ops = {}
            exp_all = {}
            nS = 0
            natt = 0
            for gI in range(_ATTN_GROUPS[0]):
                for hh in range(4):
                    h = 4 * gI + hh
                    for (qsel, cq) in ((0, p), (1, 7 - p)):
                        qoff = h * TT + qsel * 512
                        natt += 1
                        nsub = 4 * (cq + 1)
                        si = 0
                        av_first = list(ld_grp[gI])
                        if natt >= 3:
                            av_first.append(aw_ops[natt - 2])
                        if natt >= 2:
                            av_first.append(nrm_ops[natt - 1])
                        av_last = None
                        for c in range(cq + 1):
                            for j in range(4):
                                nS += 1
                                diag = (c == cq)
                                deps_s = []
                                if si == 0:
                                    deps_s = ld_grp[gI] + [q_ready[h]]
                                if nS >= 3:
                                    deps_s.append(exp_all[nS - 2])
                                s_op = P.add(
                                    "tensor",
                                    (lambda gI, c, j, qoff, bI: lambda te: te.matmul(
                                        psS[bI][:],
                                        kg[gI % 3][:, c * 512 + j * 128: c * 512 + j * 128 + 128],
                                        q_attn[:, qoff:qoff + 512],
                                        start=True, stop=True))(gI, c, j, qoff, (nS - 1) % 2),
                                    deps=deps_s)
                                pre = s_op
                                if diag:
                                    pre = P.add(
                                        "vector",
                                        (lambda j, bI: lambda v: v.tensor_tensor(
                                            psS[bI][:], psS[bI][:],
                                            mask[:, 512 - 128 * j: 1024 - 128 * j],
                                            ALU.add))(j, (nS - 1) % 2),
                                        deps=[s_op])
                                e_op = P.add(
                                    "scalar",
                                    (lambda bI, bp: lambda s: s.activation(
                                        pt[bp][:], psS[bI][:], AF.Exp, bias=b12[:],
                                        scale=SCALE))((nS - 1) % 2, (nS - 1) % 3),
                                    deps=[pre])
                                exp_all[nS] = e_op
                                av = P.add(
                                    "tensor",
                                    (lambda gI, c, j, ba, bp, st_, sp_: lambda te: [
                                        te.matmul(psA[ba][:],
                                                  vg[gI % 3][:, c * 512 + j * 128:
                                                             c * 512 + j * 128 + 128],
                                                  pt[bp][:], start=st_, stop=sp_),
                                        te.matmul(psDn[:], onescol[:], pt[bp][:],
                                                  start=st_, stop=sp_)])(
                                        gI, c, j, (natt - 1) % 2, (nS - 1) % 3,
                                        si == 0, si == nsub - 1),
                                    deps=[e_op] + (av_first if si == 0 else []))
                                av_last = av
                                si += 1
                        att_ops[natt] = av_last
                        nrm_deps = [av_last]
                        if natt >= 3:
                            nrm_deps.append(dq_ops[natt - 2])
                        nrm = P.add(
                            "vector",
                            (lambda qsel: lambda v: v.reciprocal(
                                invd[0:1, qsel * 512:qsel * 512 + 512], psDn[:]))(qsel),
                            deps=nrm_deps)
                        nrm_ops[natt] = nrm
                        dq_deps = [nrm]
                        if natt >= 2:
                            dq_deps.append(uo_ops[natt - 1])
                        dq = P.add(
                            "tensor",
                            (lambda qsel: lambda te: [te.transpose(
                                psDq[:, jq:jq + 1],
                                invd[0:1, qsel * 512 + jq * 128: qsel * 512 + jq * 128 + 128],
                                identf[0:1, 0:1]) for jq in range(4)])(qsel),
                            deps=dq_deps)
                        dq_ops[natt] = dq
                        uo_deps = [dq]
                        if natt >= 3:
                            uo_deps.append(rt_ops[(natt - 2, 3)])
                        uo = P.add(
                            "scalar",
                            (lambda bI: lambda s: s.copy(invq4[bI][:], psDq[:]))((natt - 1) % 2),
                            deps=uo_deps)
                        uo_ops[natt] = uo
                        aw_deps = [av_last]
                        if natt >= 2:
                            aw_deps.append(rt_ops[(natt - 1, 3)])
                        aw = P.add(
                            "scalar",
                            (lambda ba: lambda s: s.copy(a_un[:], psA[ba][:]))((natt - 1) % 2),
                            deps=aw_deps)
                        aw_ops[natt] = aw
                        for jq in range(4):
                            tdeps = [aw]
                            if natt > 1 or jq > 0:
                                tdeps.append(rt_ops[(natt, jq - 1)] if jq >= 1
                                             else rt_ops[(natt - 1, 3)])
                            t1 = P.add(
                                "tensor",
                                (lambda jq: lambda te: te.transpose(
                                    psT[:], a_un[:, jq * 128:jq * 128 + 128], ident[:]))(jq),
                                deps=tdeps)
                            rdeps = [t1, uo]
                            if natt > 1 or jq > 0:
                                rdeps.append(fw_ops[(natt, jq - 1)] if jq >= 1
                                             else fw_ops[(natt - 1, 3)])
                            rt = P.add(
                                "vector",
                                (lambda bI, jq: lambda v: v.tensor_scalar(
                                    a_tk[:], psT[:], invq4[bI][:, jq:jq + 1], None,
                                    ALU.mult))((natt - 1) % 2, jq),
                                deps=rdeps)
                            rt_ops[(natt, jq)] = rt
                            t2 = P.add("tensor",
                                       lambda te: te.transpose(psF[:], a_tk[:], ident[:]),
                                       deps=[rt])
                            fw = P.add(
                                "scalar",
                                (lambda qoff, jq: lambda s: s.copy(
                                    q_attn[:, qoff + jq * 128: qoff + jq * 128 + 128],
                                    psF[:]))(qoff, jq),
                                deps=[t2])
                            fw_ops[(natt, jq)] = fw
            for gI in range(3, _ATTN_GROUPS[0]):
                first_ld[gI].deps.append(att_ops[8 * (gI - 3) + 8])
            P.observe(fw_ops[(8 * _ATTN_GROUPS[0], 3)])
            attn_last = fw_ops[(8 * _ATTN_GROUPS[0], 3)]

    ph2.close()
    ph23 = contextlib.ExitStack()
    y_fm = ph23.enter_context(nc.sbuf_tensor("y_fm", [128, NDC * TT], BF16))
    wo_sb = [ph23.enter_context(nc.sbuf_tensor(f"wo_sb{i}", [128, D], BF16)) for i in range(2)]
    psY = [ph23.enter_context(nc.psum_tensor(f"psY{i}", [128, 512], F32)) for i in range(2)]

    ld_wo = {}
    wo_mm = {}
    yev = {}
    gidx = 0
    for oc in range(NDC):
        wdeps = [mm_ops[NG - 1], exch_done, rope_ops[79], attn_last]
        if _DEBUG["dump"]:
            wdeps += [d_xfm, d_trig]
        if oc >= 2:
            wdeps.append(wo_mm[2 * (oc - 1)])
        ld_wo[oc] = P.add(
            "gpsimd",
            (lambda oc: lambda g: g.dma_start(out=wo_sb[oc % 2][:], in_=wo_n[oc]))(oc),
            deps=wdeps, dma=True)
        for th in range(2):
            gidx += 1
            mdeps = [ld_wo[oc], attn_last]
            if gidx >= 3:
                mdeps.append(yev[gidx - 2])

            def f_wo(oc=oc, th=th, bI=(gidx - 1) % 2):
                def fn(te):
                    return [te.matmul(
                        psY[bI][:], wo_sb[oc % 2][:, fc * 128:(fc + 1) * 128],
                        q_attn[:, fc * TT + th * 512: fc * TT + th * 512 + 512],
                        start=(fc == 0), stop=(fc == NDC - 1)) for fc in range(NDC)]
                return fn
            wo_mm[gidx] = P.add("tensor", f_wo(), deps=mdeps)
            yev[gidx] = P.add(
                "scalar",
                (lambda oc, th, bI: lambda s: s.copy(
                    y_fm[:, oc * TT + th * 512: oc * TT + th * 512 + 512],
                    psY[bI][:]))(oc, th, (gidx - 1) % 2),
                deps=[wo_mm[gidx]])
    y_done = yev[gidx]

    ph3 = contextlib.ExitStack()
    x1acc = ph3.enter_context(nc.sbuf_tensor("x1acc", [128, D], F32))
    xcent = ph3.enter_context(nc.sbuf_tensor("xcent", [128, D], F32))
    xres = ph3.enter_context(nc.sbuf_tensor("xres", [128, D], F32))
    lncol = ph3.enter_context(nc.sbuf_tensor("lncol", [128, 6], F32))
    psX1 = [ph3.enter_context(nc.psum_tensor(f"psX1_{i}", [128, 128], BF16)) for i in range(2)]

    ln_fin = {}
    spill3 = {}
    x1tev = {}
    nt3 = 0
    for tt in range(8):
        xdeps = [attn_last] if tt == 0 else [ln_fin[tt - 1]]
        xr = P.add("sync",
                   (lambda tt: lambda sy: sy.dma_start(
                       out=xres[:], in_=x_tok[tt * 128:(tt + 1) * 128, :]))(tt),
                   deps=xdeps, dma=True)
        ev_prev = None
        for oc in range(NDC):
            nt3 += 1
            tdeps = [y_done]
            if nt3 >= 3:
                tdeps.append(x1tev[nt3 - 2])
            if tt >= 1 and oc == 0:
                tdeps.append(spill3[tt - 1])
            t3 = P.add("tensor",
                       (lambda oc, tt, bI: lambda te: te.transpose(
                           psX1[bI][:],
                           y_fm[:, oc * TT + tt * 128: oc * TT + tt * 128 + 128],
                           ident[:]))(oc, tt, (nt3 - 1) % 2),
                       deps=tdeps)
            edeps = [t3, xr]
            if tt >= 1 and oc == 0:
                edeps.append(spill3[tt - 1])
            x1tev[nt3] = P.add(
                "vector",
                (lambda oc, bI: lambda v: v.tensor_tensor(
                    x1acc[:, oc * 128:(oc + 1) * 128], psX1[bI][:],
                    xres[:, oc * 128:(oc + 1) * 128], ALU.add))(oc, (nt3 - 1) % 2),
                deps=edeps)
        ln1 = P.add(
            "vector",
            lambda v: v.reduce_sum(lncol[:, 0:1], x1acc[:], AX.X),
            deps=[x1tev[nt3]])
        ln1b = P.add(
            "scalar",
            lambda s: s.activation(lncol[:, 4:5], lncol[:, 0:1], AF.Copy,
                                   scale=-1.0 / D),
            deps=[ln1])
        ln2 = P.add(
            "vector",
            lambda v: v.tensor_scalar(xcent[:], x1acc[:], lncol[:, 4:5], None,
                                      ALU.add),
            deps=[ln1b])
        ln3 = P.add(
            "scalar",
            lambda s: s.activation(x1acc[:], xcent[:], AF.Square,
                                   accum_out=lncol[:, 1:2]),
            deps=[ln2])
        ln3b = P.add(
            "vector",
            lambda v: v.tensor_copy(scr2[:, 6:7], beps[:]),
            deps=[ln3])
        ln4 = P.add(
            "scalar",
            lambda s: s.activation(lncol[:, 2:3], lncol[:, 1:2], AF.Sqrt,
                                   bias=beps[:], scale=1.0 / D),
            deps=[ln3b])
        ln4b = P.add(
            "vector",
            lambda v: v.reciprocal(lncol[:, 3:4], lncol[:, 2:3]),
            deps=[ln4])
        ln4c = P.add(
            "scalar",
            lambda s: s.activation(scr2[:, 6:7], beps[:], AF.Copy),
            deps=[ln4b])
        ln5 = P.add(
            "vector",
            lambda v: v.tensor_scalar(x1acc[:], xcent[:], lncol[:, 3:4], None,
                                      ALU.mult),
            deps=[ln4c])
        ln_fin[tt] = ln5
        spill3[tt] = P.add(
            "scalar",
            (lambda tt: lambda s: s.dma_start(
                out=x1_dram[tt * 128:(tt + 1) * 128, :], in_=x1acc[:]))(tt),
            deps=[ln5], dma=True)

    ph3.close()
    ph23.close()
    ph12.close()
    ph4 = contextlib.ExitStack()
    x1_fm = ph4.enter_context(nc.sbuf_tensor("x1_fm", [128, NDC * TT], BF16))
    wtb = [ph4.enter_context(nc.sbuf_tensor(f"wtb{i}", [128, NDC * 512], BF16)) for i in range(2)]
    ball = ph4.enter_context(nc.sbuf_tensor("ball", [128, D], BF16))
    at_sb = ph4.enter_context(nc.sbuf_tensor("at_sb", [128, NDC * 128], BF16))
    rwt_sb = ph4.enter_context(nc.sbuf_tensor("rwt_sb", [128, NDC * 8], BF16))
    bias_sb = ph4.enter_context(nc.sbuf_tensor("bias_sb", [1, D], BF16))
    rb_sb = ph4.enter_context(nc.sbuf_tensor("rb_sb", [1, 8], F32))
    u_fm = ph4.enter_context(nc.sbuf_tensor("u_fm", [128, TT], BF16))
    x1rd = ph4.enter_context(nc.sbuf_tensor("x1rd", [128, D], F32))
    x1bf = ph4.enter_context(nc.sbuf_tensor("x1bf", [128, D], BF16))
    qacc = ph4.enter_context(nc.sbuf_tensor("qacc", [128, 2 * NDC], F32))
    qbf = ph4.enter_context(nc.sbuf_tensor("qbf", [128, NDC], BF16))
    lgp = ph4.enter_context(nc.sbuf_tensor("lgp", [128, 8], F32))
    rtg = ph4.enter_context(nc.sbuf_tensor("rtg", [1, 8], F32))
    rtgb = ph4.enter_context(nc.sbuf_tensor("rtgb", [1, 8], BF16))
    rtT = ph4.enter_context(nc.sbuf_tensor("rtT", [8, 1], BF16))
    rexp = ph4.enter_context(nc.sbuf_tensor("rexp", [128, 1], F32))
    x1sl = [ph4.enter_context(nc.sbuf_tensor(f"x1sl{i}", [128, 512], F32)) for i in range(2)]
    o2ev = [ph4.enter_context(nc.sbuf_tensor(f"o2ev{i}", [128, 512], F32)) for i in range(2)]
    psF2 = [ph4.enter_context(nc.psum_tensor(f"psF2_{i}", [128, 128], BF16)) for i in range(2)]
    psR = ph4.enter_context(nc.psum_tensor("psR", [128, 32], F32))
    psL = ph4.enter_context(nc.psum_tensor("psL", [1, 8], F32))
    psRx = ph4.enter_context(nc.psum_tensor("psRx", [128, 1], BF16))
    psTA = ph4.enter_context(nc.psum_tensor("psTA", [128, 512], F32))
    psFF = [ph4.enter_context(nc.psum_tensor(f"psFF{i}", [128, 512], F32)) for i in range(2)]

    ld4 = {}
    for name, dst, src in (("at", at_sb, a_t2), ("ball", ball, ball_in),
                           ("rwt", rwt_sb, rwt2), ("bias", bias_sb, bias_in),
                           ("rb", rb_sb, rb_in)):
        ld4[name] = P.add("gpsimd",
                          (lambda dst, src: lambda g: g.dma_start(out=dst[:], in_=src))(dst, src),
                          deps=[exch_done] + list(spill3.values()), dma=True)

    x1c = {}
    rmm = {}
    qacc_add = {}
    t4e = {}
    nT4 = 0
    zop = P.add("vector", lambda v: v.memset(qacc[:, 0:NDC], 0.0),
                deps=list(spill3.values()))
    for tt in range(8):
        rdeps = list(spill3.values())
        if tt >= 1:
            rdeps.append(x1c[tt - 1])
        rd = P.add("sync",
                   (lambda tt: lambda sy: sy.dma_start(
                       out=x1rd[:], in_=x1_dram[tt * 128:(tt + 1) * 128, :]))(tt),
                   deps=rdeps, dma=True)
        cdeps = [rd]
        if tt >= 1:
            cdeps.append(rmm[tt - 1])
        x1c[tt] = P.add("vector", lambda v: v.tensor_copy(x1bf[:], x1rd[:]), deps=cdeps)
        for dc in range(NDC):
            nT4 += 1
            tdeps = [x1c[tt]]
            if nT4 >= 3:
                tdeps.append(t4e[nT4 - 2])
            t4 = P.add("tensor",
                       (lambda dc, bI: lambda te: te.transpose(
                           psF2[bI][:], x1bf[:, dc * 128:(dc + 1) * 128], ident[:]))(
                               dc, (nT4 - 1) % 2),
                       deps=tdeps)
            t4e[nT4] = P.add(
                "scalar",
                (lambda dc, tt, bI: lambda s: s.copy(
                    x1_fm[:, dc * TT + tt * 128: dc * TT + tt * 128 + 128], psF2[bI][:]))(
                        dc, tt, (nT4 - 1) % 2),
                deps=[t4])
        rdeps2 = [x1c[tt], c_vec, t4e[nT4]]
        if tt >= 1:
            rdeps2.append(qacc_add[tt - 1])

        def f_rmm(tt=tt):
            def fn(te):
                return [te.matmul(psR[:, dc:dc + 1], x1bf[:, dc * 128:(dc + 1) * 128],
                                  qm8[:, tt:tt + 1], start=True, stop=True)
                        for dc in range(NDC)]
            return fn
        rmm[tt] = P.add("tensor", f_rmm(), deps=rdeps2)
        qacc_add[tt] = P.add(
            "vector",
            (lambda tt: lambda v: v.tensor_tensor(
                qacc[:, NDC * ((tt + 1) % 2):NDC * ((tt + 1) % 2) + NDC],
                qacc[:, NDC * (tt % 2):NDC * (tt % 2) + NDC], psR[:], ALU.add))(tt),
            deps=[rmm[tt], zop])

    qbf_b = P.add("scalar", lambda s: s.activation(
        scr2[:, 5:6], beps[:], AF.Copy), deps=[qacc_add[7]])
    qbf_op = P.add("vector", lambda v: v.tensor_copy(qbf[:], qacc[:, 0:NDC]), deps=[qbf_b])
    lg_mm = P.add(
        "tensor",
        lambda te: [te.matmul(psL[:], qbf[:, dc:dc + 1], rwt_sb[:, dc * 8:(dc + 1) * 8],
                              start=(dc == 0), stop=(dc == NDC - 1)) for dc in range(NDC)],
        deps=[qbf_op, ld4["rwt"]])
    lgz = P.add("vector", lambda v: v.memset(lgp[:], 0.0), deps=list(spill3.values()))
    lg_ev = P.add("scalar", lambda s: s.copy(lgp[0:1, :], psL[:]), deps=[lg_mm, lgz])
    P.observe(lg_ev)

    def f_rbc(g):
        nm, val = lg_ev.sem
        g.wait_ge(sems[nm], val)
        r = g.partition_id()
        for k in g.Switch(r, 8):
            g.remote_dma_broadcast(rsl[:, 8 * k:8 * k + 8], lgp[:],
                                   sems["rsem"], sems["lsem"],
                                   rdests=[(0, d) for d in range(8)]).then_inc(sems["prep"], 1)
        g.wait_ge(sems["prep"], 2 * NWAVE + 1)
        g.trigger_dma(count=1)
        g.wait_ge(sems["rsem"], 16)
        return g.engine_nop()
    rbc = P.add("gpsimd", f_rbc, deps=[])
    P.observe(rbc)

    with P.switch(8, ["vector"], 8) as body:
        rsum_done = None
        for k8 in range(8):
            body(k8)
            bb = k8 // 4
            rsA = P.add(
                "vector",
                (lambda bb: lambda v: [
                    v.tensor_tensor(rtg2[:], rsl[0:1, 32 * bb:32 * bb + 8],
                                    rsl[0:1, 32 * bb + 8:32 * bb + 16], ALU.add),
                    v.tensor_tensor(rtg[:], rsl[0:1, 32 * bb + 16:32 * bb + 24],
                                    rsl[0:1, 32 * bb + 24:32 * bb + 32], ALU.add)])(bb),
                deps=[rbc])
            rsB = P.add("scalar", lambda s: s.activation(
                scr2[:, 5:6], beps[:], AF.Copy), deps=[rsA])
            rs_op = P.add(
                "vector",
                lambda v: v.tensor_tensor(lgp[0:1, 0:8], rtg2[:], rtg[:], ALU.add),
                deps=[rsB])
            P.observe(rs_op)
            if rsum_done is None:
                rsum_done = rs_op

    sm1 = P.add(
        "scalar",
        lambda s: s.activation(rtg2[:], lgp[0:1, 0:8], AF.Copy,
                               scale=1.0 / float(QE - QS)),
        deps=[rsum_done])
    sm1b = P.add(
        "vector",
        lambda v: v.tensor_tensor(rtg[:], rtg2[:], rb_sb[:], ALU.add),
        deps=[sm1, ld4["rb"]])
    sm1c = P.add("scalar", lambda s: s.activation(
        scr2[:, 5:6], beps[:], AF.Copy), deps=[sm1b])
    sm1d = P.add(
        "vector",
        lambda v: v.reduce_max(scr[0:1, 0:1], rtg[:], AX.X, negate=True),
        deps=[sm1c])
    sm2 = P.add("scalar",
                lambda s: s.activation(rtg2[:], rtg[:], AF.Exp, bias=scr[0:1, 0:1], scale=1.0),
                deps=[sm1d])
    sm2b = P.add(
        "vector",
        lambda v: v.reduce_sum(scr[0:1, 1:2], rtg2[:], AX.X),
        deps=[sm2])
    sm2b2 = P.add("scalar", lambda s: s.activation(
        scr2[:, 5:6], beps[:], AF.Copy), deps=[sm2b])
    sm2c = P.add(
        "vector",
        lambda v: v.reciprocal(scr[0:1, 2:3], scr[0:1, 1:2]),
        deps=[sm2b2])
    sm2d = P.add("scalar", lambda s: s.activation(
        scr2[:, 5:6], beps[:], AF.Copy), deps=[sm2c])
    sm3 = P.add(
        "vector",
        lambda v: [v.tensor_scalar(rtgb[:], rtg2[:], scr[0:1, 2:3], None, ALU.mult),
                   v.tensor_scalar(rtg[:], rtg2[:], scr[0:1, 2:3], None, ALU.mult)],
        deps=[sm2d])
    rx1 = P.add("tensor",
                lambda te: te.transpose(psRx[0:8, 0:1], rtgb[:], ident[0:1, 0:1]),
                deps=[sm3])
    rx2 = P.add("scalar", lambda s: s.copy(rtT[:], psRx[0:8, 0:1]), deps=[rx1])
    rx3 = P.add("tensor",
                lambda te: te.matmul(psFF[0][:, 0:1], selm[:], rtT[:], start=True, stop=True),
                deps=[rx2])
    rx4 = P.add("scalar", lambda s: s.mul(rexp[:], psFF[0][:, 0:1], 2.0), deps=[rx3])

    u_op = {}
    for th in range(2):
        tdeps = [t4e[nT4], ld4["at"]]
        if th == 1:
            tdeps.append(u_op[0])

        def f_ta(th=th):
            def fn(te):
                return [te.matmul(psTA[:], at_sb[:, dc * 128:(dc + 1) * 128],
                                  x1_fm[:, dc * TT + th * 512: dc * TT + th * 512 + 512],
                                  start=(dc == 0), stop=(dc == NDC - 1)) for dc in range(NDC)]
            return fn
        ta = P.add("tensor", f_ta(), deps=tdeps)
        u_op[th] = P.add(
            "vector",
            (lambda th: lambda v: v.tensor_scalar(
                u_fm[:, th * 512:(th + 1) * 512], psTA[:], rexp[:], None, ALU.mult))(th),
            deps=[ta, rx4])

    ld_wt = {}
    ffn_mm = {}
    o2e = {}
    o2s = {}
    idx4 = 0
    for osl in range(8):
        wdeps = [exch_done] + list(spill3.values())
        if osl >= 2:
            wdeps.append(ffn_mm[(osl - 2, 7)])
        ld_wt[osl] = P.add(
            "gpsimd",
            (lambda osl: lambda g: g.dma_start(out=wtb[osl % 2][:], in_=wt_l[osl]))(osl),
            deps=wdeps, dma=True)
        for tt in range(8):
            idx4 += 1
            ldeps = list(spill3.values())
            if idx4 >= 3:
                ldeps.append(o2e[idx4 - 2])
            xl = P.add(
                "sync",
                (lambda tt, osl, bI: lambda sy: sy.dma_start(
                    out=x1sl[bI][:],
                    in_=x1_dram[tt * 128:(tt + 1) * 128, osl * 512:(osl + 1) * 512]))(
                        tt, osl, (idx4 - 1) % 2),
                deps=ldeps, dma=True)
            mdeps = [ld_wt[osl], u_op[1], t4e[nT4], ld4["ball"], ld4["bias"]]
            if idx4 >= 3:
                mdeps.append(o2e[idx4 - 2])

            def f_ffn(tt=tt, osl=osl, bI=(idx4 - 1) % 2):
                def fn(te):
                    r = [te.matmul(
                        psFF[bI][:], x1_fm[:, dc * TT + tt * 128: dc * TT + tt * 128 + 128],
                        wtb[osl % 2][:, dc * 512:(dc + 1) * 512],
                        start=(dc == 0), stop=False) for dc in range(NDC)]
                    r.append(te.matmul(psFF[bI][:], u_fm[:, tt * 128:(tt + 1) * 128],
                                       ball[:, osl * 512:(osl + 1) * 512],
                                       start=False, stop=False))
                    r.append(te.matmul(psFF[bI][:], onesrow[:],
                                       bias_sb[0:1, osl * 512:(osl + 1) * 512],
                                       start=False, stop=True))
                    return r
                return fn
            ffn_mm[(osl, tt)] = P.add("tensor", f_ffn(), deps=mdeps)
            o2e[idx4] = P.add(
                "vector",
                (lambda bI: lambda v: v.tensor_tensor(o2ev[bI][:], psFF[bI][:], x1sl[bI][:],
                                                      ALU.add))((idx4 - 1) % 2),
                deps=[ffn_mm[(osl, tt)], xl])
            o2s[(osl, tt)] = P.add(
                "scalar",
                (lambda tt, osl, bI: lambda s: s.dma_start(
                    out=o2_dram[tt * 128:(tt + 1) * 128, osl * 512:(osl + 1) * 512],
                    in_=o2ev[bI][:]))(tt, osl, (idx4 - 1) % 2),
                deps=[o2e[idx4]], dma=True)

    ph4.close()
    ph5 = contextlib.ExitStack()
    o2r = ph5.enter_context(nc.sbuf_tensor("o2r", [128, D], F32))
    oc5 = ph5.enter_context(nc.sbuf_tensor("oc5", [128, D], F32))
    osq = ph5.enter_context(nc.sbuf_tensor("osq", [128, D], F32))
    lnc2 = ph5.enter_context(nc.sbuf_tensor("lnc2", [128, 6], F32))

    out_ops = []
    prev_l3 = None
    prev_wr = None
    for tt in range(8):
        deps = list(o2s.values())
        if prev_l3 is not None:
            deps.append(prev_l3)
        rd5 = P.add("sync",
                    (lambda tt: lambda sy: sy.dma_start(
                        out=o2r[:], in_=o2_dram[tt * 128:(tt + 1) * 128, :]))(tt),
                    deps=deps, dma=True)
        l1d = [rd5]
        if prev_wr is not None:
            l1d.append(prev_wr)
        l1 = P.add(
            "vector",
            lambda v: v.reduce_sum(lnc2[:, 0:1], o2r[:], AX.X),
            deps=l1d)
        l1b = P.add(
            "scalar",
            lambda s: s.activation(lnc2[:, 4:5], lnc2[:, 0:1], AF.Copy,
                                   scale=-1.0 / D),
            deps=[l1])
        l2 = P.add(
            "vector",
            lambda v: v.tensor_scalar(oc5[:], o2r[:], lnc2[:, 4:5], None, ALU.add),
            deps=[l1b])
        l2b = P.add(
            "scalar",
            lambda s: s.activation(o2r[:], oc5[:], AF.Square,
                                   accum_out=lnc2[:, 1:2]),
            deps=[l2])
        l2c = P.add(
            "vector",
            lambda v: v.tensor_copy(scr2[:, 7:8], beps[:]),
            deps=[l2b])
        l2d = P.add(
            "scalar",
            lambda s: s.activation(lnc2[:, 2:3], lnc2[:, 1:2], AF.Sqrt,
                                   bias=beps[:], scale=1.0 / D),
            deps=[l2c])
        l2e = P.add(
            "vector",
            lambda v: v.reciprocal(lnc2[:, 3:4], lnc2[:, 2:3]),
            deps=[l2d])
        l2f = P.add(
            "scalar",
            lambda s: s.activation(scr2[:, 7:8], beps[:], AF.Copy),
            deps=[l2e])
        l3 = P.add(
            "vector",
            lambda v: v.tensor_scalar(osq[:], oc5[:], lnc2[:, 3:4], None, ALU.mult),
            deps=[l2f])
        wr = P.add("scalar",
                   (lambda tt: lambda s: s.dma_start(
                       out=out_h[tt * 128:(tt + 1) * 128, :], in_=osq[:]))(tt),
                   deps=[l3], dma=True)
        out_ops.append(wr)
        prev_l3 = l3
        prev_wr = wr

    if _DEBUG["dump"]:
        P.add("gpsimd", lambda g: g.dma_start(out=dbg["kall"], in_=k_all[:, :, :]),
              deps=[exch_done], dma=True)
        P.add("gpsimd", lambda g: g.dma_start(out=dbg["vall"], in_=v_all[:, :]),
              deps=[exch_done], dma=True)
        P.add("gpsimd", lambda g: g.dma_start(out=dbg["attn"], in_=q_attn[:]),
              deps=[attn_last], dma=True)
        P.add("gpsimd", lambda g: g.dma_start(out=dbg["x1"], in_=x1_dram[:, :]),
              deps=[spill3[7]], dma=True)
        P.add("gpsimd", lambda g: g.dma_start(out=dbg["rout"], in_=rtg[:]),
              deps=[sm3, out_ops[-1]], dma=True)

    P.add("gpsimd", lambda g: g.engine_nop(), deps=list(out_ops))

    if trunc is not None:
        kept = set()
        if trunc >= 1000:
            for seg in P.segments:
                d = seg["ops"] if seg["kind"] == "plain" else None
                if d is None:
                    for b in seg["bodies"]:
                        for e in ENGS:
                            b[e] = [o for o in b[e] if o.order < trunc - 1000]
                else:
                    for e in ENGS:
                        d[e] = [o for o in d[e] if o.order < trunc - 1000]
        else:
            P.segments = P.segments[:trunc]
        for seg in P.segments:
            dicts = [seg["ops"]] if seg["kind"] == "plain" else seg["bodies"]
            for dd in dicts:
                for e in ENGS:
                    for o in dd[e]:
                        kept.add(id(o))
        for seg in P.segments:
            dicts = [seg["ops"]] if seg["kind"] == "plain" else seg["bodies"]
            for dd in dicts:
                for e in ENGS:
                    for o in dd[e]:
                        o.deps = [dp for dp in o.deps if id(dp) in kept]
    P.schedule()
    P.emit(nc, blk, sems)
    ph5.close()
    st.close()
    return nc, dbg


def _shard_inputs(inputs):
    X = np.asarray(inputs["L0_attn_X"], np.float32)
    Wq = np.asarray(inputs["L0_attn_Wq"], np.float32)
    Wk = np.asarray(inputs["L0_attn_Wk"], np.float32)
    Wv = np.asarray(inputs["L0_attn_Wv"], np.float32)
    Wo = np.asarray(inputs["L0_attn_Wo"], np.float32)
    W = np.asarray(inputs["L0_ffn_W"], np.float32)
    b = np.asarray(inputs["L0_ffn_b"], np.float32)
    A = np.asarray(inputs["L0_ffn_A"], np.float32)
    B = np.asarray(inputs["L0_ffn_B"], np.float32)
    rW = np.asarray(inputs["L0_ffn_router_W"], np.float32)
    rb = np.asarray(inputs["L0_ffn_router_b"], np.float32)

    wq_l = np.ascontiguousarray(Wq.reshape(NDC, 128, NH, 128).transpose(2, 1, 0, 3)
                                ).reshape(NH, 128, D)
    wk_l = np.ascontiguousarray(Wk.reshape(NDC, 128, NKV, 128).transpose(2, 1, 0, 3)
                                ).reshape(NKV, 128, D)
    wv_l = np.ascontiguousarray(Wv.reshape(NDC, 128, NKV, 128).transpose(2, 1, 0, 3)
                                ).reshape(NKV, 128, D)
    wo_n = np.ascontiguousarray(Wo.reshape(NDC, 128, NDC, 128).transpose(2, 1, 0, 3)
                                ).reshape(NDC, 128, D)
    wt_l = np.ascontiguousarray(W.reshape(8, 512, NDC, 128).transpose(0, 3, 2, 1)
                                ).reshape(8, 128, NDC * 512)
    A_all = A.reshape(128, D)
    a_t2 = np.ascontiguousarray(A_all.T.reshape(NDC, 128, 128).transpose(1, 0, 2)
                                ).reshape(128, NDC * 128)
    ball = np.ascontiguousarray(B.transpose(0, 2, 1).reshape(128, D))
    rwt2 = np.ascontiguousarray(rW.T.reshape(NDC, 128, 8).transpose(1, 0, 2)
                                ).reshape(128, NDC * 8)
    bias_row = np.ascontiguousarray(b.reshape(1, D))
    rb_row = np.ascontiguousarray(rb.reshape(1, 8))

    in_maps = []
    for c in range(8):
        bb = c // 4
        p = c % 4
        rows = np.r_[512 * p:512 * (p + 1), 512 * (7 - p):512 * (8 - p)]
        xc = np.ascontiguousarray(X[bb][rows])
        xfm = np.ascontiguousarray(xc.T)
        pos = rows.astype(np.float32)
        pos_c = np.ascontiguousarray(pos.reshape(8, 128).T)
        posb = np.ascontiguousarray(np.broadcast_to(pos.reshape(1, TT), (128, TT)))
        in_maps.append({
            "x_tok": xc, "x_fmT": xfm,
            "wq_l": wq_l, "wk_l": wk_l, "wv_l": wv_l, "wo_n": wo_n,
            "wt_l": wt_l, "a_t2": a_t2, "ball_in": ball, "bias_in": bias_row,
            "rwt2": rwt2, "rb_in": rb_row, "pos_in": pos_c, "posb_in": posb,
        })
    return in_maps


def kernel(**inputs):
    in_maps = _shard_inputs(inputs)
    nc, dbg = build_program()
    nc.finalize()
    res = run_bass_kernel_spmd(nc, in_maps, core_ids=list(range(8)))
    out = np.zeros((2, T, D), np.float32)
    for c in range(8):
        bb = c // 4
        p = c % 4
        o = res.results[c]["out_h"]
        out[bb, 512 * p:512 * (p + 1)] = o[0:512]
        out[bb, 512 * (7 - p):512 * (8 - p)] = o[512:1024]
    if _DEBUG["dump"]:
        kernel._last_debug = res.results
    return out
